# version 11
# baseline (speedup 1.0000x reference)
import math
from contextlib import ExitStack

import numpy as np
import ml_dtypes

import concourse.bass as bass
import concourse.mybir as mybir
from concourse.bass_utils import run_bass_kernel_spmd

F32 = mybir.dt.float32
BF16 = mybir.dt.bfloat16
AF = mybir.ActivationFunctionType
ALU = mybir.AluOpType
AX = mybir.AxisListType
BFNP = ml_dtypes.bfloat16

NCORES = 8
SEQ = 2
L = 2048
D = 1024
DC = 8
H = 16
DH = 64
TPS = 16
T = SEQ * L
NT = T // 128
INTER = 2048
NE = 8
GT = 8
NG = NT // GT
LN_EPS = 1e-5

ENGINES = ("tensor", "vector", "scalar", "gpsimd", "sync")

MERGED_MASK = True
MERGED_VSB = True
POOL_NORM = True
PIPELINED_A = True
A_TILES = None
A_CUT = 9


class Builder:

    def __init__(self):
        self.ops = []
        self.last_w = {}
        self.readers = {}
        self.need_inc = set()
        self.dma_chan = {}
        self.rec = []

    def record(self):
        self.rec.append([])

    def take(self):
        return self.rec.pop()

    def interleave(self, *bufs):
        items = []
        for b in bufs:
            n = len(b)
            for i, it in enumerate(b):
                items.append(((i + 0.5) / n, it))
        items.sort(key=lambda x: x[0])
        for _, (engine, fn, reads, writes, dma) in items:
            self.op(engine, fn, reads=reads, writes=writes, dma=dma)

    def op(self, engine, fn, reads=(), writes=(), dma=False):
        if self.rec:
            self.rec[-1].append((engine, fn, tuple(reads), tuple(writes), dma))
            return -1
        i = len(self.ops)
        chan = None
        if dma:
            key = writes[0] if writes else (reads[0] if reads else "misc")
            chan = self.dma_chan.setdefault(key, len(self.dma_chan))
        deps = set()
        for k in reads:
            lw = self.last_w.get(k)
            if lw is not None:
                deps.add(lw)
        for k in writes:
            lw = self.last_w.get(k)
            if lw is not None:
                deps.add(lw)
            for r in self.readers.get(k, ()):
                deps.add(r)
        deps.discard(i)
        kept = set()
        for d in deps:
            od = self.ops[d]
            if (engine == "tensor" and od["engine"] == "tensor"
                    and not dma and not od["dma"]):
                continue
            kept.add(d)
            self.need_inc.add(d)
        for k in reads:
            self.readers.setdefault(k, []).append(i)
        for k in writes:
            self.last_w[k] = i
            self.readers[k] = []
        self.ops.append(dict(engine=engine, fn=fn, deps=kept, dma=dma,
                             chan=chan, kind="op"))
        return i

    def barrier(self):
        seen = set()
        for i in range(len(self.ops) - 1, -1, -1):
            o = self.ops[i]
            if o["kind"] != "op":
                break
            s = ("D%d" % o["chan"]) if o["dma"] else o["engine"]
            if s not in seen:
                seen.add(s)
                self.need_inc.add(i)
        self.ops.append(dict(engine=None, fn=None, deps=None, dma=False,
                             kind="barrier"))

    def finalize(self, nc):
        self.barrier()
        streams = list(ENGINES) + ["D%d" % c for c in range(len(self.dma_chan))]
        sidx = {s: j for j, s in enumerate(streams)}
        cnt = {s: 0 for s in streams}
        inc_no = {}
        bar_tot = {}
        for i, o in enumerate(self.ops):
            if o["kind"] == "barrier":
                bar_tot[i] = dict(cnt)
                continue
            s = ("D%d" % o["chan"]) if o["dma"] else o["engine"]
            if o["dma"]:
                cnt[s] += 16
                inc_no[i] = (s, cnt[s])
            elif i in self.need_inc:
                cnt[s] += 1
                inc_no[i] = (s, cnt[s])
        nS = len(streams)
        evc = {e: [0] * nS for e in ENGINES}
        ccq = {s: [0] * nS for s in streams}
        opvc = {}
        sched = {e: [] for e in ENGINES}
        for i, o in enumerate(self.ops):
            if o["kind"] == "barrier":
                tot = bar_tot[i]
                for e in ENGINES:
                    waits = []
                    for s in streams:
                        if tot[s] > evc[e][sidx[s]]:
                            waits.append((s, tot[s]))
                    sched[e].append((waits, None))
                    vc = evc[e]
                    for s in streams:
                        if tot[s] > vc[sidx[s]]:
                            vc[sidx[s]] = tot[s]
                continue
            e = o["engine"]
            vc = evc[e]
            need = {}
            for d in o["deps"]:
                s, v = inc_no[d]
                j = sidx[s]
                if vc[j] < v and need.get(s, 0) < v:
                    need[s] = v
            waits = sorted(need.items())
            for d in o["deps"]:
                dv = opvc.get(d)
                if dv is not None:
                    for j in range(nS):
                        if dv[j] > vc[j]:
                            vc[j] = dv[j]
            mys = ("D%d" % o["chan"]) if o["dma"] else e
            mine = inc_no.get(i)
            if mine is not None:
                myvc = [max(a, b) for a, b in zip(vc, ccq[mys])]
                myvc[sidx[mine[0]]] = max(myvc[sidx[mine[0]]], mine[1])
                opvc[i] = myvc
                ccq[mys] = [max(a, b) for a, b in zip(ccq[mys], myvc)]
            sched[e].append((waits, i))
        with ExitStack() as stk:
            sems = {s: stk.enter_context(nc.semaphore("sem_" + s))
                    for s in streams
                    if cnt[s] > 0 or any(w[0] == s
                                         for e in ENGINES
                                         for ws, _ in sched[e] for w in ws)}
            block = stk.enter_context(nc.Block())

            def mk_body(e):
                def body(eng):
                    for waits, i in sched[e]:
                        for s, v in waits:
                            eng.wait_ge(sems[s], v)
                        if i is None:
                            continue
                        o = self.ops[i]
                        inst = o["fn"](eng)
                        mine = inc_no.get(i)
                        if mine is not None:
                            inst.then_inc(sems[mine[0]], 16 if o["dma"] else 1)
                return body

            for e in ENGINES:
                getattr(block, e)(mk_body(e))


class St:

    def __init__(self, nc, bld):
        self.nc = nc
        self.b = bld
        self.stk = None
        self.t = {}

    prefix = ""

    def sb(self, name, shape, dtype):
        nm = self.prefix + name
        self.t[nm] = self.stk.enter_context(
            self.nc.sbuf_tensor(nm, shape, dtype))
        return self.t[nm]

    def ps(self, name, shape, dtype=F32):
        nm = self.prefix + name
        self.t[nm] = self.stk.enter_context(
            self.nc.psum_tensor(nm, shape, dtype))
        return self.t[nm]

    def dma(self, q, out, in_, reads=(), writes=()):
        self.b.op(q, lambda e: e.dma_start(out=out, in_=in_),
                  reads=reads, writes=writes, dma=True)

    def mm(self, out, lhsT, rhs, start, stop, reads=(), writes=()):
        self.b.op("tensor",
                  lambda e: e.matmul(out, lhsT, rhs, start=start, stop=stop),
                  reads=reads, writes=writes)

    def tr(self, out, in_, ident, reads=(), writes=()):
        self.b.op("tensor", lambda e: e.transpose(out, in_, ident),
                  reads=reads, writes=writes)

    def v(self, meth, *a, reads=(), writes=(), **kw):
        self.b.op("vector", lambda e: getattr(e, meth)(*a, **kw),
                  reads=reads, writes=writes)

    def s(self, meth, *a, reads=(), writes=(), **kw):
        self.b.op("scalar", lambda e: getattr(e, meth)(*a, **kw),
                  reads=reads, writes=writes)


def pk(name, lo, hi, itemsize=4):
    per_bank = 2048 // itemsize
    return [("P" + name, c) for c in range(lo // per_bank,
                                           (hi - 1) // per_bank + 1)]


def emit_ln(st, xt_ap, xkeys, stt, stkey, sq_ap, sqkey, xnb_ap, xnkey):
    st.v("reduce_sum", stt[:, 0:1], xt_ap, axis=AX.X,
         reads=xkeys, writes=[stkey])
    st.s("activation", sq_ap, xt_ap, AF.Square, reads=xkeys, writes=[sqkey])
    st.v("reduce_sum", stt[:, 1:2], sq_ap, axis=AX.X,
         reads=[sqkey], writes=[stkey])
    st.v("tensor_scalar", stt[:, 2:3], stt[:, 0:1], 1.0 / D, None, ALU.mult,
         reads=[stkey], writes=[stkey])
    st.v("tensor_scalar", stt[:, 3:4], stt[:, 1:2], 1.0 / D, LN_EPS,
         ALU.mult, ALU.add, reads=[stkey], writes=[stkey])
    st.s("activation", stt[:, 4:5], stt[:, 2:3], AF.Square,
         reads=[stkey], writes=[stkey])
    st.v("tensor_sub", stt[:, 5:6], stt[:, 3:4], stt[:, 4:5],
         reads=[stkey], writes=[stkey])
    st.s("activation", stt[:, 6:7], stt[:, 5:6], AF.Ln,
         reads=[stkey], writes=[stkey])
    st.s("activation", stt[:, 7:8], stt[:, 6:7], AF.Exp, scale=-0.5,
         reads=[stkey], writes=[stkey])
    st.v("tensor_scalar", stt[:, 8:9], stt[:, 2:3], stt[:, 7:8], -1.0,
         ALU.mult, ALU.mult, reads=[stkey], writes=[stkey])
    st.s("activation", xnb_ap, xt_ap, AF.Identity,
         bias=stt[:, 8:9], scale=stt[:, 7:8],
         reads=list(xkeys) + [stkey], writes=[xnkey])


def emit_tp8(st, src_ap, srckey, dst_view3, dstkeys, pt, identb):
    for c in range(DC):
        st.tr(pt[:, c * 128:(c + 1) * 128], src_ap[:, c * 128:(c + 1) * 128],
              identb[:, :], reads=[srckey, "identb"], writes=[("PPT", 0)])
    st.v("tensor_copy", dst_view3,
         pt[:, :].rearrange("p (c t) -> p c t", c=8),
         reads=[("PPT", 0)], writes=dstkeys)


def emit_softmax_top2(st, lg_ap, lgkeys, gsc, gkey, cmb_ap, cmbkey,
                      bias_ap=None, biaskey=None):
    st.v("reduce_max", gsc[:, 0:1], lg_ap, axis=AX.X, negate=True,
         reads=lgkeys, writes=[gkey])
    st.s("activation", gsc[:, 8:16], lg_ap, AF.Exp, bias=gsc[:, 0:1],
         reads=list(lgkeys) + [gkey], writes=[gkey])
    st.v("reduce_sum", gsc[:, 1:2], gsc[:, 8:16], axis=AX.X,
         reads=[gkey], writes=[gkey])
    st.v("reciprocal", gsc[:, 2:3], gsc[:, 1:2], reads=[gkey], writes=[gkey])
    st.v("tensor_scalar", gsc[:, 16:24], gsc[:, 8:16], gsc[:, 2:3], None,
         ALU.mult, reads=[gkey], writes=[gkey])
    if bias_ap is not None:
        st.v("tensor_add", gsc[:, 24:32], gsc[:, 16:24], bias_ap,
             reads=[gkey, biaskey], writes=[gkey])
        bsl = gsc[:, 24:32]
    else:
        bsl = gsc[:, 16:24]
    st.v("reduce_max", gsc[:, 3:4], bsl, axis=AX.X, reads=[gkey], writes=[gkey])
    st.v("tensor_scalar", gsc[:, 32:40], bsl, gsc[:, 3:4], None, ALU.is_equal,
         reads=[gkey], writes=[gkey])
    st.v("scalar_tensor_tensor", gsc[:, 40:48], gsc[:, 32:40], -1e30, bsl,
         ALU.mult, ALU.add, reads=[gkey], writes=[gkey])
    st.v("reduce_max", gsc[:, 4:5], gsc[:, 40:48], axis=AX.X,
         reads=[gkey], writes=[gkey])
    st.v("tensor_scalar", gsc[:, 48:56], bsl, gsc[:, 4:5], None, ALU.is_ge,
         reads=[gkey], writes=[gkey])
    st.v("tensor_mul", cmb_ap, gsc[:, 16:24], gsc[:, 48:56],
         reads=[gkey], writes=[cmbkey])


def bcast(ap2, h, rep):
    import copy
    v = ap2.rearrange("p (h o) -> p h o", o=1)
    v = copy.deepcopy(v)
    v.ap[-1] = [0, rep]
    return v


def emit_stage_a(st, xi, xo, wq, wk, wv, wo, wb, bb, consts):
    nc, b = st.nc, st.b
    identb, idf, triu, ones1f, zbf, obf = consts
    SLAB = 4
    NSL = NT // SLAB
    with ExitStack() as stk:
        st.stk = stk
        wqa = st.sb("wqa", [128, 8192], BF16)
        wka = st.sb("wka", [128, 8192], BF16)
        wva = st.sb("wva", [128, 8192], BF16)
        woa = st.sb("woa", [128, 8192], BF16)
        wba = st.sb("wba", [128, 128], BF16)
        bbs = st.sb("bbs", [1, 16], F32)
        xt = st.sb("xt", [128, 4096], BF16)
        xrs = st.sb("xrs", [128, 1024], F32)
        stt = st.sb("stt", [128, 128], F32)
        sq = st.sb("sq", [128, 4096], BF16)
        sqk = st.sb("sqk", [128, 4096], BF16)
        sqln = st.sb("sqln", [128, 4096], BF16)
        xnb = st.sb("xnb", [128, 4096], BF16)
        xnf = st.sb("xnf", [128, 2 * 4096], BF16)
        qf = st.sb("qf", [128, 4096], BF16)
        kf = st.sb("kf", [128, 4096], BF16)
        ssq = st.sb("ssq", [128, 256], F32)
        qnb = st.sb("qnb", [128, 4096], BF16)
        knb = st.sb("knb", [128, 4096], BF16)
        vsb = st.sb("vsb", [128, 2 * 4096], BF16)
        qfm = st.sb("qfm", [128, 4096], BF16)
        kfm = st.sb("kfm", [128, 4096], BF16)
        amb = st.sb("amb", [128, 2048], BF16)
        ssb = st.sb("ssb", [128, 1024], BF16)
        ssf = st.sb("ssf", [128, 512], F32)
        afm = st.sb("afm", [128, 1024], BF16)
        pa = st.ps("pa", [128, 1024])
        pb = st.ps("pb", [128, 1024])
        psm = st.ps("psm", [128, 512])
        s0 = st.ps("s0", [128, 512])
        pt = st.ps("pt", [128, 1024], BF16)
        pt2 = st.ps("pt2", [128, 1024], BF16)
        skey = ("Ps0", 0)

        st.dma("gpsimd", wqa[:, :].rearrange("p (c n) -> p c n", c=8),
               wq.rearrange("(c p) n -> p c n", p=128), writes=["wqa"])
        st.dma("gpsimd", wka[:, :].rearrange("p (c n) -> p c n", c=8),
               wk.rearrange("(c p) n -> p c n", p=128), writes=["wka"])
        st.dma("gpsimd", wva[:, :].rearrange("p (c n) -> p c n", c=8),
               wv.rearrange("(c p) n -> p c n", p=128), writes=["wva"])
        st.dma("gpsimd", woa[:, :].rearrange("p (c n) -> p c n", c=8),
               wo.rearrange("(c p) n -> p c n", p=128), writes=["woa"])
        st.dma("gpsimd", wba[:, :].rearrange("p (c n) -> p c n", c=8),
               wb.rearrange("(c p) n -> p c n", p=128), writes=["wba"])
        st.dma("sync", bbs[:, :], bb[:, :], writes=["bbs"])

        def p1a(s):
            r = s % 2
            bq = (s * SLAB) // TPS
            t0 = (s * SLAB) % TPS
            row0 = bq * L + t0 * 128
            xk = ("xt",)
            xsl = xt[:, :]
            st.dma("gpsimd", xsl.rearrange("p (t d) -> p t d", t=4),
                   xi[row0:row0 + 512, :].rearrange("(t p) d -> p t d", p=128),
                   writes=[xk])
            so = r * 64
            stk_ = ("stt", r)
            v3 = xsl.rearrange("p (t d) -> p t d", d=1024)
            st.v("reduce_sum", stt[:, so:so + 4], v3, axis=AX.X,
                 reads=[xk], writes=[stk_])
            st.s("activation", sqln[:, :], xsl, AF.Square,
                 reads=[xk], writes=[("sqln",)])
            st.v("reduce_sum", stt[:, so + 4:so + 8],
                 sqln[:, :].rearrange("p (t d) -> p t d", d=1024), axis=AX.X,
                 reads=[("sqln",)], writes=[stk_])
            st.v("tensor_scalar", stt[:, so + 8:so + 12], stt[:, so:so + 4],
                 1.0 / D, None, ALU.mult, reads=[stk_], writes=[stk_])
            st.v("tensor_scalar", stt[:, so + 12:so + 16],
                 stt[:, so + 4:so + 8], 1.0 / D, LN_EPS, ALU.mult, ALU.add,
                 reads=[stk_], writes=[stk_])
            st.s("activation", stt[:, so + 16:so + 20], stt[:, so + 8:so + 12],
                 AF.Square, reads=[stk_], writes=[stk_])
            st.v("tensor_sub", stt[:, so + 20:so + 24], stt[:, so + 12:so + 16],
                 stt[:, so + 16:so + 20], reads=[stk_], writes=[stk_])
            st.s("activation", stt[:, so + 24:so + 28], stt[:, so + 20:so + 24],
                 AF.Ln, reads=[stk_], writes=[stk_])
            st.s("activation", stt[:, so + 28:so + 32], stt[:, so + 24:so + 28],
                 AF.Exp, scale=-0.5, reads=[stk_], writes=[stk_])
            st.v("scalar_tensor_tensor", stt[:, so + 32:so + 36],
                 stt[:, so + 8:so + 12], -1.0, stt[:, so + 28:so + 32],
                 ALU.mult, ALU.mult, reads=[stk_], writes=[stk_])
            for t in range(SLAB):
                st.s("activation", xnb[:, t * 1024:(t + 1) * 1024],
                     xsl[:, t * 1024:(t + 1) * 1024], AF.Identity,
                     bias=stt[:, so + 32 + t:so + 33 + t],
                     scale=stt[:, so + 28 + t:so + 29 + t],
                     reads=[xk, stk_], writes=[("xnb",)])
            for t in range(SLAB):
                emit_tp8(st, xnb[:, t * 1024:(t + 1) * 1024], ("xnb",),
                         xnf[:, r * 4096 + t * 1024:r * 4096 + (t + 1) * 1024]
                         .rearrange("p (c t) -> p c t", c=8),
                         [("xnf", r)], pt, identb)

        def p1b(s):
            r = s % 2
            xfk = ("xnf", r)

            def lhs(t, c):
                return xnf[:, r * 4096 + t * 1024 + c * 128:
                           r * 4096 + t * 1024 + (c + 1) * 128]

            for t in range(SLAB):
                for c in range(DC):
                    st.mm(pa[:, 0:512], lhs(t, c),
                          wqa[:, c * 1024:c * 1024 + 512],
                          c == 0, c == DC - 1,
                          reads=[xfk, "wqa"], writes=pk("pa", 0, 512))
                    st.mm(pa[:, 512:1024], lhs(t, c),
                          wqa[:, c * 1024 + 512:(c + 1) * 1024],
                          c == 0, c == DC - 1,
                          reads=[xfk, "wqa"], writes=pk("pa", 512, 1024))
                for c in range(DC):
                    st.mm(pb[:, 0:512], lhs(t, c),
                          wka[:, c * 1024:c * 1024 + 512],
                          c == 0, c == DC - 1,
                          reads=[xfk, "wka"], writes=pk("pb", 0, 512))
                    st.mm(pb[:, 512:1024], lhs(t, c),
                          wka[:, c * 1024 + 512:(c + 1) * 1024],
                          c == 0, c == DC - 1,
                          reads=[xfk, "wka"], writes=pk("pb", 512, 1024))
                for c in range(DC):
                    st.mm(psm[:, t * 16:(t + 1) * 16], lhs(t, c),
                          wba[:, c * 16:(c + 1) * 16], c == 0, False,
                          reads=[xfk, "wba"], writes=pk("psm", 0, 64))
                st.mm(psm[:, t * 16:(t + 1) * 16], ones1f[0:1, :], bbs[0:1, :],
                      False, True, reads=["ones1f", "bbs"],
                      writes=pk("psm", 0, 64))
                st.v("tensor_copy", qf[:, t * 1024:(t + 1) * 1024], pa[:, :],
                     reads=pk("pa", 0, 1024), writes=[("qf",)])
                st.s("copy", kf[:, t * 1024:(t + 1) * 1024], pb[:, :],
                     reads=pk("pb", 0, 1024), writes=[("kf",)])
                for c in range(DC):
                    st.mm(pa[:, 0:512], lhs(t, c),
                          wva[:, c * 1024:c * 1024 + 512],
                          c == 0, c == DC - 1,
                          reads=[xfk, "wva"], writes=pk("pa", 0, 512))
                    st.mm(pa[:, 512:1024], lhs(t, c),
                          wva[:, c * 1024 + 512:(c + 1) * 1024],
                          c == 0, c == DC - 1,
                          reads=[xfk, "wva"], writes=pk("pa", 512, 1024))
                st.v("tensor_copy",
                     vsb[:, r * 4096 + t * 1024:r * 4096 + (t + 1) * 1024],
                     pa[:, :],
                     reads=pk("pa", 0, 1024), writes=[("vsb0", r)])

        def p1c(s):
            r = s % 2
            kb_, kq_, kk_ = ("ssq", "b"), ("ssq", "q"), ("ssq", "k")
            b.record()
            st.s("activation", ssq[:, 192:256], psm[:, 0:64],
                 AF.Exp, scale=-1.0, reads=pk("psm", 0, 64), writes=[kb_])
            st.v("tensor_scalar", ssq[:, 192:256], ssq[:, 192:256],
                 1.0, None, ALU.add, reads=[kb_], writes=[kb_])
            st.v("reciprocal", ssq[:, 128:192], ssq[:, 192:256],
                 reads=[kb_], writes=[kb_])
            st.b.op("gpsimd", lambda e, _r=r: e.tensor_tensor(
                vsb[:, _r * 4096:(_r + 1) * 4096]
                .rearrange("p (h e) -> p h e", e=64),
                vsb[:, _r * 4096:(_r + 1) * 4096]
                .rearrange("p (h e) -> p h e", e=64),
                bcast(ssq[:, 128:192], 64, 64), ALU.mult),
                reads=[("vsb0", r), kb_], writes=[("vsb", r)])
            chain_b = b.take()
            b.record()
            st.s("activation", sq[:, :], qf[:, :], AF.Square,
                 reads=[("qf",)], writes=[("sq",)])
            st.v("reduce_sum", ssq[:, 0:64],
                 sq[:, :].rearrange("p (h e) -> p h e", e=64), axis=AX.X,
                 reads=[("sq",)], writes=[kq_])
            st.s("activation", ssq[:, 0:64], ssq[:, 0:64],
                 AF.Ln, reads=[kq_], writes=[kq_])
            st.s("activation", ssq[:, 0:64], ssq[:, 0:64],
                 AF.Exp, scale=-0.5, reads=[kq_], writes=[kq_])
            st.b.op("gpsimd", lambda e: e.tensor_tensor(
                qnb[:, :].rearrange("p (h e) -> p h e", e=64),
                qf[:, :].rearrange("p (h e) -> p h e", e=64),
                bcast(ssq[:, 0:64], 64, 64), ALU.mult),
                reads=[("qf",), kq_], writes=[("qnb",)])
            chain_q = b.take()
            b.record()
            st.s("activation", sqk[:, :], kf[:, :], AF.Square,
                 reads=[("kf",)], writes=[("sqk",)])
            st.v("reduce_sum", ssq[:, 64:128],
                 sqk[:, :].rearrange("p (h e) -> p h e", e=64), axis=AX.X,
                 reads=[("sqk",)], writes=[kk_])
            st.s("activation", ssq[:, 64:128], ssq[:, 64:128],
                 AF.Ln, reads=[kk_], writes=[kk_])
            st.s("activation", ssq[:, 64:128], ssq[:, 64:128],
                 AF.Exp, scale=-0.5, reads=[kk_], writes=[kk_])
            st.b.op("gpsimd", lambda e: e.tensor_tensor(
                knb[:, :].rearrange("p (h e) -> p h e", e=64),
                kf[:, :].rearrange("p (h e) -> p h e", e=64),
                bcast(ssq[:, 64:128], 64, 64), ALU.mult),
                reads=[("kf",), kk_], writes=[("knb",)])
            chain_k = b.take()
            b.interleave(chain_b, chain_q, chain_k)
            for t in range(SLAB):
                emit_tp8(st, qnb[:, t * 1024:(t + 1) * 1024], ("qnb",),
                         qfm[:, t * 1024:(t + 1) * 1024]
                         .rearrange("p (c t) -> p c t", c=8),
                         [("qfm",)], pt2, identb)
            for t in range(SLAB):
                emit_tp8(st, knb[:, t * 1024:(t + 1) * 1024], ("knb",),
                         kfm[:, t * 1024:(t + 1) * 1024]
                         .rearrange("p (c t) -> p c t", c=8),
                         [("kfm",)], pt2, identb)

        def p2(s, t):
            g = s * SLAB + t
            bq, tt = divmod(g, TPS)
            ra = g % 2
            rs = s % 2
            row0 = bq * L + tt * 128
            off = rs * 4096 + t * 1024
            if tt == 0:
                st.b.op("gpsimd", lambda e: e.memset(ssf[:, :], 0.0),
                        reads=[("ssf",)], writes=[("ssf",)])
            for h in [hh for hh in range(H) if hh % 2 == 0] + \
                     [hh for hh in range(H) if hh % 2 == 1]:
                par = h % 2
                hc = h // 2
                rl = par * 64
                dst = pa if h < 8 else pb
                dnm = "pa" if h < 8 else "pb"
                col = (h % 8) * 128
                st.mm(dst[:, col:col + 128],
                      kfm[rl:rl + 64, t * 1024 + hc * 128:
                          t * 1024 + (hc + 1) * 128],
                      qfm[rl:rl + 64, t * 1024 + hc * 128:
                          t * 1024 + (hc + 1) * 128],
                      True, True,
                      reads=[("kfm",), ("qfm",)],
                      writes=pk(dnm, col, col + 128))
            for h in range(H):
                par = h % 2
                hc = h // 2
                rl = par * 64
                st.mm(s0[rl:rl + 64, hc * 64:(hc + 1) * 64],
                      knb[:, t * 1024 + h * 64:t * 1024 + (h + 1) * 64],
                      vsb[:, off + h * 64:off + (h + 1) * 64],
                      True, True,
                      reads=[("knb",), ("vsb", rs)], writes=[skey])
            st.v("tensor_tensor",
                 amb[:, 0:1024].rearrange("p (h l) -> p h l", l=128),
                 pa[:, :].rearrange("p (h l) -> p h l", l=128),
                 bcast_cols(triu[:, :], 8), ALU.mult,
                 reads=pk("pa", 0, 1024) + ["triu"], writes=[("amb", 0)])
            st.v("tensor_tensor",
                 amb[:, 1024:2048].rearrange("p (h l) -> p h l", l=128),
                 pb[:, :].rearrange("p (h l) -> p h l", l=128),
                 bcast_cols(triu[:, :], 8), ALU.mult,
                 reads=pk("pb", 0, 1024) + ["triu"], writes=[("amb", 1)])
            for h in range(H):
                par = h % 2
                hc = h // 2
                rl = par * 64
                okeys = pk("pa", hc * 128, (hc + 1) * 128)
                st.mm(pa[rl:rl + 64, hc * 128:(hc + 1) * 128],
                      vsb[:, off + h * 64:off + (h + 1) * 64],
                      amb[:, (h % 8) * 128 + (0 if h < 8 else 1024):
                          (h % 8) * 128 + 128 + (0 if h < 8 else 1024)],
                      True, tt == 0,
                      reads=[("vsb", rs), ("amb", 0 if h < 8 else 1)],
                      writes=okeys)
                if tt > 0:
                    st.mm(pa[rl:rl + 64, hc * 128:(hc + 1) * 128],
                          ssb[rl:rl + 64,
                              (tt % 2) * 512 + hc * 64:
                              (tt % 2) * 512 + (hc + 1) * 64],
                          qfm[rl:rl + 64, t * 1024 + hc * 128:
                              t * 1024 + (hc + 1) * 128],
                          False, True,
                          reads=[("ssb", tt % 2), ("qfm",)], writes=okeys)
            if tt + 1 < TPS:
                st.v("tensor_add", ssf[:, :], ssf[:, :], s0[:, :],
                     reads=[skey, ("ssf",)], writes=[("ssf",)])
                st.v("tensor_copy",
                     ssb[:, ((tt + 1) % 2) * 512:((tt + 1) % 2 + 1) * 512],
                     ssf[:, :], reads=[("ssf",)],
                     writes=[("ssb", (tt + 1) % 2)])
            st.v("tensor_copy", afm[:, :], pa[:, :],
                 reads=pk("pa", 0, 1024), writes=[("afm",)])
            for c in range(DC):
                st.mm(pb[:, 0:512],
                      afm[:, c * 128:(c + 1) * 128],
                      woa[:, c * 1024:c * 1024 + 512], c == 0, c == DC - 1,
                      reads=[("afm",), "woa"], writes=pk("pb", 0, 512))
                st.mm(pb[:, 512:1024],
                      afm[:, c * 128:(c + 1) * 128],
                      woa[:, c * 1024 + 512:(c + 1) * 1024],
                      c == 0, c == DC - 1,
                      reads=[("afm",), "woa"], writes=pk("pb", 512, 1024))
            xra = xrs[:, :]
            st.dma("sync", xra, xi[row0:row0 + 128, :], writes=[("xrs",)])
            st.v("tensor_add", xra, xra, pb[:, :],
                 reads=[("xrs",)] + pk("pb", 0, 1024), writes=[("xrs",)])
            st.dma("scalar", xo[row0:row0 + 128, :], xra,
                   reads=[("xrs",)])

        NSLT = (A_TILES // SLAB) if A_TILES else NSL
        p1a(0)
        p1b(0)
        p1c(0)
        p1a(1)
        for s in range(NSLT):
            if s + 1 < NSLT:
                p1b(s + 1)
            for t in range(SLAB):
                p2(s, t)
            b.record()
            if s + 1 < NSLT:
                p1c(s + 1)
            buf1 = b.take()
            b.record()
            if s + 2 < NSLT:
                p1a(s + 2)
            buf2 = b.take()
            if buf1 or buf2:
                b.interleave(buf1, buf2)


def bcast_cols(ap2, rep):
    import copy
    v = copy.deepcopy(ap2)
    n = v.ap[-1]
    v.ap = v.ap[:-1] + [[0, rep], n]
    return v


def emit_stage_b(st, xi, xo, w1t, w2t, w3t, sw1t, sw2t, sw3t, gwt, gb, consts,
                 experts=tuple(range(NE + 1))):
    nc, b = st.nc, st.b
    identb, idf, triu, ones1f, zbf, obf = consts
    with ExitStack() as stk:
        st.stk = stk
        w1q = st.sb("w1q", [128, 2 * 4096], BF16)
        w3q = st.sb("w3q", [128, 2 * 4096], BF16)
        w2b = st.sb("w2b", [128, 16384], BF16)
        gws = st.sb("gws", [128, 64], BF16)
        gbs = st.sb("gbs", [1, 8], F32)
        gbx = st.sb("gbx", [128, 8], F32)
        x1s = st.sb("x1s", [128, 8192], F32)
        stt = st.sb("stt", [128, 40], F32)
        sqln = st.sb("sqln", [128, 4096], BF16)
        xnb = st.sb("xnb", [128, 4096], BF16)
        xnf = st.sb("xnf", [128, 8192], BF16)
        lgt = st.sb("lgt", [128, 64], F32)
        gsc = st.sb("gsc", [128, 320], F32)
        cmb = st.sb("cmb", [128, 64], F32)
        hsl = st.sb("hsl", [128, 1024], F32)
        hb = st.sb("hb", [128, 16384], BF16)
        pa = st.ps("pa", [128, 1024])
        pb = st.ps("pb", [128, 1024])
        psm = st.ps("psm", [128, 512])
        pt = st.ps("pt", [128, 1024], BF16)

        st.dma("gpsimd", gws[:, :].rearrange("p (c n) -> p c n", c=8),
               gwt.rearrange("(c p) n -> p c n", p=128), writes=["gws"])
        st.dma("sync", gbs[:, :], gb[:, :], writes=["gbs"])
        st.mm(psm[:, 0:8], ones1f[0:1, :], gbs[0:1, :], True, True,
              reads=["ones1f", "gbs"], writes=pk("psm", 0, 8))
        st.v("tensor_copy", gbx[:, :], psm[:, 0:8],
             reads=pk("psm", 0, 8), writes=["gbx"])

        def wsrc(e, which):
            if e < NE:
                return {"w1": w1t, "w2": w2t, "w3": w3t}[which][e]
            return {"w1": sw1t, "w2": sw2t, "w3": sw3t}[which]

        def slab_ln(g, sl):
            xsl = x1s[:, sl * 4096:(sl + 1) * 4096]
            xks = [("x1s", sl * 4 + i) for i in range(4)]
            stk_ = ("stt",)
            st.v("reduce_sum", stt[:, 0:4],
                 xsl.rearrange("p (t d) -> p t d", d=1024), axis=AX.X,
                 reads=xks, writes=[stk_])
            st.s("activation", sqln[:, :], xsl, AF.Square,
                 reads=xks, writes=[("sqln",)])
            st.v("reduce_sum", stt[:, 4:8],
                 sqln[:, :].rearrange("p (t d) -> p t d", d=1024), axis=AX.X,
                 reads=[("sqln",)], writes=[stk_])
            st.v("tensor_scalar", stt[:, 8:12], stt[:, 0:4],
                 1.0 / D, None, ALU.mult, reads=[stk_], writes=[stk_])
            st.v("tensor_scalar", stt[:, 12:16], stt[:, 4:8],
                 1.0 / D, LN_EPS, ALU.mult, ALU.add, reads=[stk_],
                 writes=[stk_])
            st.s("activation", stt[:, 16:20], stt[:, 8:12], AF.Square,
                 reads=[stk_], writes=[stk_])
            st.v("tensor_sub", stt[:, 20:24], stt[:, 12:16], stt[:, 16:20],
                 reads=[stk_], writes=[stk_])
            st.s("activation", stt[:, 24:28], stt[:, 20:24], AF.Ln,
                 reads=[stk_], writes=[stk_])
            st.s("activation", stt[:, 28:32], stt[:, 24:28], AF.Exp,
                 scale=-0.5, reads=[stk_], writes=[stk_])
            st.v("scalar_tensor_tensor", stt[:, 32:36], stt[:, 8:12], -1.0,
                 stt[:, 28:32], ALU.mult, ALU.mult, reads=[stk_],
                 writes=[stk_])
            for i in range(4):
                t = sl * 4 + i
                st.s("activation", xnb[:, i * 1024:(i + 1) * 1024],
                     xsl[:, i * 1024:(i + 1) * 1024], AF.Identity,
                     bias=stt[:, 32 + i:33 + i], scale=stt[:, 28 + i:29 + i],
                     reads=xks + [stk_], writes=[("xnb",)])
            for i in range(4):
                t = sl * 4 + i
                emit_tp8(st, xnb[:, i * 1024:(i + 1) * 1024], ("xnb",),
                         xnf[:, :].rearrange("p (c t) -> p c t", c=8)
                         [:, :, t * 128:(t + 1) * 128],
                         [("xnf",)], pt, identb)

        for g in range(NG):
            row0 = g * 1024
            st.dma("sync",
                   x1s[:, :].rearrange("p (t d) -> p t d", t=8),
                   xi[row0:row0 + 1024, :].rearrange("(t p) d -> p t d", p=128),
                   writes=[("x1s", t) for t in range(GT)])
            slab_ln(g, 0)
            slab_ln(g, 1)
            def lhs(t, c):
                return xnf[:, c * 1024 + t * 128:c * 1024 + (t + 1) * 128]

            for t in range(GT):
                for c in range(DC):
                    st.mm(psm[:, t * 8:(t + 1) * 8], lhs(t, c),
                          gws[:, c * 8:(c + 1) * 8], c == 0, c == DC - 1,
                          reads=[("xnf",), "gws"], writes=pk("psm", 0, 64))
            st.v("tensor_copy", lgt[:, :], psm[:, 0:64],
                 reads=pk("psm", 0, 64), writes=[("lgt",)])
            gk = ("gsc",)
            v3 = lgt[:, :].rearrange("p (t e) -> p t e", e=8)
            st.v("reduce_max", gsc[:, 0:8], v3, axis=AX.X, negate=True,
                 reads=[("lgt",)], writes=[gk])
            st.v("tensor_tensor", gsc[:, 64:128]
                 .rearrange("p (t e) -> p t e", e=8), v3,
                 bcast(gsc[:, 0:8], 8, 8), ALU.add,
                 reads=[("lgt",), gk], writes=[gk])
            st.s("activation", gsc[:, 64:128], gsc[:, 64:128], AF.Exp,
                 reads=[gk], writes=[gk])
            st.v("reduce_sum", gsc[:, 8:16], gsc[:, 64:128]
                 .rearrange("p (t e) -> p t e", e=8), axis=AX.X,
                 reads=[gk], writes=[gk])
            st.v("reciprocal", gsc[:, 16:24], gsc[:, 8:16],
                 reads=[gk], writes=[gk])
            st.v("tensor_tensor", gsc[:, 128:192]
                 .rearrange("p (t e) -> p t e", e=8),
                 gsc[:, 64:128].rearrange("p (t e) -> p t e", e=8),
                 bcast(gsc[:, 16:24], 8, 8), ALU.mult,
                 reads=[gk], writes=[gk])
            st.v("tensor_tensor", gsc[:, 192:256]
                 .rearrange("p (t e) -> p t e", e=8),
                 gsc[:, 128:192].rearrange("p (t e) -> p t e", e=8),
                 bcast_cols(gbx[:, :], 8), ALU.add,
                 reads=[gk, "gbx"], writes=[gk])
            st.v("reduce_max", gsc[:, 24:32], gsc[:, 192:256]
                 .rearrange("p (t e) -> p t e", e=8), axis=AX.X,
                 reads=[gk], writes=[gk])
            st.v("tensor_tensor", gsc[:, 256:320]
                 .rearrange("p (t e) -> p t e", e=8),
                 gsc[:, 192:256].rearrange("p (t e) -> p t e", e=8),
                 bcast(gsc[:, 24:32], 8, 8), ALU.is_equal,
                 reads=[gk], writes=[gk])
            st.v("scalar_tensor_tensor", gsc[:, 256:320], gsc[:, 256:320],
                 -1e30, gsc[:, 192:256], ALU.mult, ALU.add,
                 reads=[gk], writes=[gk])
            st.v("reduce_max", gsc[:, 32:40], gsc[:, 256:320]
                 .rearrange("p (t e) -> p t e", e=8), axis=AX.X,
                 reads=[gk], writes=[gk])
            st.v("tensor_tensor", gsc[:, 256:320]
                 .rearrange("p (t e) -> p t e", e=8),
                 gsc[:, 192:256].rearrange("p (t e) -> p t e", e=8),
                 bcast(gsc[:, 32:40], 8, 8), ALU.is_ge,
                 reads=[gk], writes=[gk])
            st.v("tensor_mul", cmb[:, :], gsc[:, 128:192], gsc[:, 256:320],
                 reads=[gk], writes=[("cmb",)])
            for e in experts:
                def load_quarter(qg):
                    rot = qg % 2
                    st.dma("gpsimd",
                           w1q[:, rot * 4096:(rot + 1) * 4096]
                           .rearrange("p (c i) -> p c i", c=8),
                           wsrc(e, "w1")[:, qg * 512:(qg + 1) * 512]
                           .rearrange("(c p) i -> p c i", p=128),
                           writes=[("w1q", rot)])
                    st.dma("gpsimd",
                           w3q[:, rot * 4096:(rot + 1) * 4096]
                           .rearrange("p (c i) -> p c i", c=8),
                           wsrc(e, "w3")[:, qg * 512:(qg + 1) * 512]
                           .rearrange("(c p) i -> p c i", p=128),
                           writes=[("w3q", rot)])
                load_quarter(0)
                load_quarter(1)
                st.dma("gpsimd",
                       w2b[:, :].rearrange("p (c o) -> p c o", c=16),
                       wsrc(e, "w2").rearrange("(c p) o -> p c o", p=128),
                       writes=["w2b"])
                for qg in range(4):
                    rot = qg % 2
                    if qg >= 2:
                        load_quarter(qg)
                    for sg in range(2):
                        for i4 in range(4):
                            ic = qg * 4 + i4
                            pr = ic % 2
                            ga = pa[:, pr * 512:(pr + 1) * 512]
                            gb_ = pb[:, pr * 512:(pr + 1) * 512]
                            gak = pk("pa", pr * 512, (pr + 1) * 512)
                            gbk = pk("pb", pr * 512, (pr + 1) * 512)
                            io = i4 * 128
                            for c in range(DC):
                                st.mm(ga,
                                      w1q[:, rot * 4096 + c * 512 + io:
                                          rot * 4096 + c * 512 + io + 128],
                                      xnf[:, c * 1024 + sg * 512:
                                          c * 1024 + (sg + 1) * 512],
                                      c == 0, c == DC - 1,
                                      reads=[("w1q", rot), ("xnf",)],
                                      writes=gak)
                            for c in range(DC):
                                st.mm(gb_,
                                      w3q[:, rot * 4096 + c * 512 + io:
                                          rot * 4096 + c * 512 + io + 128],
                                      xnf[:, c * 1024 + sg * 512:
                                          c * 1024 + (sg + 1) * 512],
                                      c == 0, c == DC - 1,
                                      reads=[("w3q", rot), ("xnf",)],
                                      writes=gbk)
                            st.s("activation",
                                 hsl[:, pr * 512:(pr + 1) * 512],
                                 ga, AF.Silu, reads=gak, writes=[("hsl", pr)])
                            st.v("tensor_tensor",
                                 hb[:, (sg * 16 + ic) * 512:
                                    (sg * 16 + ic + 1) * 512],
                                 hsl[:, pr * 512:(pr + 1) * 512], gb_,
                                 ALU.mult,
                                 reads=[("hsl", pr)] + gbk,
                                 writes=[("hb", sg, ic)])
                for t in range(GT):
                    sgt = t // 4
                    toff = (t % 4) * 128
                    for oh in range(2):
                        yrot = (t * 2 + oh) % 4
                        py = (pa if yrot < 2 else pb)
                        yo = (yrot % 2) * 512
                        ya = py[:, yo:yo + 512]
                        yk = pk("pa" if yrot < 2 else "pb", yo, yo + 512)
                        for ic in range(16):
                            sgi = sgt * 16 + ic
                            st.mm(ya,
                                  hb[:, sgi * 512 + toff:sgi * 512 + toff + 128],
                                  w2b[:, ic * 1024 + oh * 512:
                                      ic * 1024 + (oh + 1) * 512],
                                  ic == 0, ic == 15,
                                  reads=[("hb", sgt, ic), "w2b"], writes=yk)
                        xsl = x1s[:, t * 1024 + oh * 512:
                                  t * 1024 + (oh + 1) * 512]
                        if e < NE:
                            st.v("scalar_tensor_tensor", xsl, ya,
                                 cmb[:, t * 8 + e:t * 8 + e + 1], xsl,
                                 ALU.mult, ALU.add,
                                 reads=yk + [("cmb",), ("x1s", t)],
                                 writes=[("x1s", t)])
                        else:
                            st.v("tensor_add", xsl, xsl, ya,
                                 reads=yk + [("x1s", t)],
                                 writes=[("x1s", t)])
            st.dma("scalar",
                   xo[row0:row0 + 1024, :].rearrange("(t p) d -> p t d", p=128),
                   x1s[:, :].rearrange("p (t d) -> p t d", t=8),
                   reads=[("x1s", t) for t in range(GT)])


def emit_stage_c(st, xi, xo, wq, wk, wv, wo, bq, bk, bv, bo, m_in, consts):
    nc, b = st.nc, st.b
    identb, idf, triu, ones1f, zbf, obf = consts
    SLAB = 4
    SPS = TPS // SLAB
    NSL = NT // SLAB
    with ExitStack() as stk:
        st.stk = stk
        wqa = st.sb("wqa", [128, 8192], BF16)
        wka = st.sb("wka", [128, 8192], BF16)
        wva = st.sb("wva", [128, 8192], BF16)
        woa = st.sb("woa", [128, 8192], BF16)
        bsx = st.sb("bsx", [1, 4096], BF16)
        ms = st.sb("ms", [1, 64], F32)
        scb = st.sb("scb", [128, 16], F32)
        xt = st.sb("xt", [128, 4096], BF16)
        xrs = st.sb("xrs", [128, 1024], F32)
        stt = st.sb("stt", [128, 40], F32)
        sqln = st.sb("sqln", [128, 4096], BF16)
        sq = st.sb("sq", [128, 4096], BF16)
        sqk = st.sb("sqk", [128, 4096], BF16)
        xnb = st.sb("xnb", [128, 4096], BF16)
        xnf = st.sb("xnf", [128, 4096], BF16)
        qf = st.sb("qf", [128, 4096], BF16)
        kf = st.sb("kf", [128, 4096], BF16)
        ssq = st.sb("ssq", [128, 128], F32)
        qnb = st.sb("qnb", [128, 4096], BF16)
        knb = st.sb("knb", [128, 4096], BF16)
        vbb = st.sb("vbb", [128, 4096], BF16)
        qfa = st.sb("qfa", [128, 16384], BF16)
        ssc = st.sb("ssc", [128, 512], BF16)
        ssf = st.sb("ssf", [128, 512], F32)
        afm = st.sb("afm", [128, 1024], BF16)
        pa = st.ps("pa", [128, 1024])
        pb = st.ps("pb", [128, 1024])
        psm = st.ps("psm", [128, 512])
        s0 = st.ps("s0", [128, 512])
        pt = st.ps("pt", [128, 1024], BF16)
        pt2 = st.ps("pt2", [128, 1024], BF16)
        skey = ("Ps0c", 0)

        st.dma("gpsimd", wqa[:, :].rearrange("p (c n) -> p c n", c=8),
               wq.rearrange("(c p) n -> p c n", p=128), writes=["wqa"])
        st.dma("gpsimd", wka[:, :].rearrange("p (c n) -> p c n", c=8),
               wk.rearrange("(c p) n -> p c n", p=128), writes=["wka"])
        st.dma("gpsimd", wva[:, :].rearrange("p (c n) -> p c n", c=8),
               wv.rearrange("(c p) n -> p c n", p=128), writes=["wva"])
        st.dma("gpsimd", woa[:, :].rearrange("p (c n) -> p c n", c=8),
               wo.rearrange("(c p) n -> p c n", p=128), writes=["woa"])
        st.dma("gpsimd", bsx[0:1, 0:1024], bq[:, :], writes=["bsx"])
        st.dma("gpsimd", bsx[0:1, 1024:2048], bk[:, :], writes=["bsx"])
        st.dma("gpsimd", bsx[0:1, 2048:3072], bv[:, :], writes=["bsx"])
        st.dma("gpsimd", bsx[0:1, 3072:4096], bo[:, :], writes=["bsx"])
        st.dma("sync", ms[0:1, 0:16], m_in[:, :], writes=["ms"])
        st.s("activation", ms[0:1, 16:32], ms[0:1, 0:16], AF.Exp, scale=-1.0,
             reads=["ms"], writes=["ms"])
        st.v("tensor_scalar", ms[0:1, 16:32], ms[0:1, 16:32], 1.0, None,
             ALU.add, reads=["ms"], writes=["ms"])
        st.v("reciprocal", ms[0:1, 32:48], ms[0:1, 16:32],
             reads=["ms"], writes=["ms"])
        st.s("activation", ms[0:1, 48:64], ms[0:1, 32:48], AF.Exp,
             scale=-float(math.log(L)), reads=["ms"], writes=["ms"])
        st.mm(psm[:, 0:16], ones1f[0:1, :], ms[0:1, 48:64], True, True,
              reads=["ones1f", "ms"], writes=pk("psm", 0, 16))
        st.v("tensor_copy", scb[:, :], psm[:, 0:16],
             reads=pk("psm", 0, 16), writes=["scb"])

        def p1a(s):
            r = 0
            bqi = (s * SLAB) // TPS
            t0 = (s * SLAB) % TPS
            row0 = bqi * L + t0 * 128
            xk = ("xt",)
            xsl = xt[:, :]
            st.dma("gpsimd", xsl.rearrange("p (t d) -> p t d", t=4),
                   xi[row0:row0 + 512, :].rearrange("(t p) d -> p t d", p=128),
                   writes=[xk])
            so = 0
            stk_ = ("stt",)
            st.v("reduce_sum", stt[:, so:so + 4],
                 xsl.rearrange("p (t d) -> p t d", d=1024), axis=AX.X,
                 reads=[xk], writes=[stk_])
            st.s("activation", sqln[:, :], xsl, AF.Square,
                 reads=[xk], writes=[("sqln",)])
            st.v("reduce_sum", stt[:, so + 4:so + 8],
                 sqln[:, :].rearrange("p (t d) -> p t d", d=1024), axis=AX.X,
                 reads=[("sqln",)], writes=[stk_])
            st.v("tensor_scalar", stt[:, so + 8:so + 12], stt[:, so:so + 4],
                 1.0 / D, None, ALU.mult, reads=[stk_], writes=[stk_])
            st.v("tensor_scalar", stt[:, so + 12:so + 16],
                 stt[:, so + 4:so + 8], 1.0 / D, LN_EPS, ALU.mult, ALU.add,
                 reads=[stk_], writes=[stk_])
            st.s("activation", stt[:, so + 16:so + 20], stt[:, so + 8:so + 12],
                 AF.Square, reads=[stk_], writes=[stk_])
            st.v("tensor_sub", stt[:, so + 20:so + 24], stt[:, so + 12:so + 16],
                 stt[:, so + 16:so + 20], reads=[stk_], writes=[stk_])
            st.s("activation", stt[:, so + 24:so + 28], stt[:, so + 20:so + 24],
                 AF.Ln, reads=[stk_], writes=[stk_])
            st.s("activation", stt[:, so + 28:so + 32], stt[:, so + 24:so + 28],
                 AF.Exp, scale=-0.5, reads=[stk_], writes=[stk_])
            st.v("scalar_tensor_tensor", stt[:, so + 32:so + 36],
                 stt[:, so + 8:so + 12], -1.0, stt[:, so + 28:so + 32],
                 ALU.mult, ALU.mult, reads=[stk_], writes=[stk_])
            for t in range(SLAB):
                st.s("activation", xnb[:, t * 1024:(t + 1) * 1024],
                     xsl[:, t * 1024:(t + 1) * 1024], AF.Identity,
                     bias=stt[:, so + 32 + t:so + 33 + t],
                     scale=stt[:, so + 28 + t:so + 29 + t],
                     reads=[xk, stk_], writes=[("xnb",)])
            for t in range(SLAB):
                emit_tp8(st, xnb[:, t * 1024:(t + 1) * 1024], ("xnb",),
                         xnf[:, t * 1024:(t + 1) * 1024]
                         .rearrange("p (c t) -> p c t", c=8),
                         [("xnf",)], pt, identb)

        def p1b(s):
            xfk = ("xnf",)

            def lhs(t, c):
                return xnf[:, t * 1024 + c * 128:t * 1024 + (c + 1) * 128]

            for t in range(SLAB):
                for half in range(2):
                    for c in range(DC):
                        st.mm(pa[:, half * 512:(half + 1) * 512], lhs(t, c),
                              wqa[:, c * 1024 + half * 512:
                                  c * 1024 + (half + 1) * 512],
                              c == 0, False,
                              reads=[xfk, "wqa"],
                              writes=pk("pa", half * 512, (half + 1) * 512))
                    st.mm(pa[:, half * 512:(half + 1) * 512], obf[0:1, 0:128],
                          bsx[0:1, half * 512:(half + 1) * 512], False, True,
                          reads=["obf", "bsx"],
                          writes=pk("pa", half * 512, (half + 1) * 512))
                for half in range(2):
                    for c in range(DC):
                        st.mm(pb[:, half * 512:(half + 1) * 512], lhs(t, c),
                              wka[:, c * 1024 + half * 512:
                                  c * 1024 + (half + 1) * 512],
                              c == 0, False,
                              reads=[xfk, "wka"],
                              writes=pk("pb", half * 512, (half + 1) * 512))
                    st.mm(pb[:, half * 512:(half + 1) * 512], obf[0:1, 0:128],
                          bsx[0:1, 1024 + half * 512:1024 + (half + 1) * 512],
                          False, True,
                          reads=["obf", "bsx"],
                          writes=pk("pb", half * 512, (half + 1) * 512))
                st.v("tensor_copy", qf[:, t * 1024:(t + 1) * 1024], pa[:, :],
                     reads=pk("pa", 0, 1024), writes=[("qf",)])
                st.s("copy", kf[:, t * 1024:(t + 1) * 1024], pb[:, :],
                     reads=pk("pb", 0, 1024), writes=[("kf",)])
                for half in range(2):
                    for c in range(DC):
                        st.mm(pa[:, half * 512:(half + 1) * 512], lhs(t, c),
                              wva[:, c * 1024 + half * 512:
                                  c * 1024 + (half + 1) * 512],
                              c == 0, False,
                              reads=[xfk, "wva"],
                              writes=pk("pa", half * 512, (half + 1) * 512))
                    st.mm(pa[:, half * 512:(half + 1) * 512], obf[0:1, 0:128],
                          bsx[0:1, 2048 + half * 512:2048 + (half + 1) * 512],
                          False, True,
                          reads=["obf", "bsx"],
                          writes=pk("pa", half * 512, (half + 1) * 512))
                st.v("tensor_copy", vbb[:, t * 1024:(t + 1) * 1024], pa[:, :],
                     reads=pk("pa", 0, 1024), writes=[("vbb",)])

        def p1c(s):
            bqi = (s * SLAB) // TPS
            t0 = (s * SLAB) % TPS
            kq_, kk_ = ("ssq", "q"), ("ssq", "k")
            b.record()
            st.s("activation", sq[:, :], qf[:, :], AF.Square,
                 reads=[("qf",)], writes=[("sq",)])
            st.v("reduce_sum", ssq[:, 0:64],
                 sq[:, :].rearrange("p (h e) -> p h e", e=64), axis=AX.X,
                 reads=[("sq",)], writes=[kq_])
            st.s("activation", ssq[:, 0:64], ssq[:, 0:64], AF.Ln,
                 reads=[kq_], writes=[kq_])
            st.s("activation", ssq[:, 0:64], ssq[:, 0:64], AF.Exp, scale=-0.5,
                 reads=[kq_], writes=[kq_])
            st.b.op("gpsimd", lambda e: e.tensor_tensor(
                qnb[:, :].rearrange("p (h e) -> p h e", e=64),
                qf[:, :].rearrange("p (h e) -> p h e", e=64),
                bcast(ssq[:, 0:64], 64, 64), ALU.mult),
                reads=[("qf",), kq_], writes=[("qnb",)])
            cq = b.take()
            b.record()
            st.s("activation", sqk[:, :], kf[:, :], AF.Square,
                 reads=[("kf",)], writes=[("sqk",)])
            st.v("reduce_sum", ssq[:, 64:128],
                 sqk[:, :].rearrange("p (h e) -> p h e", e=64), axis=AX.X,
                 reads=[("sqk",)], writes=[kk_])
            st.s("activation", ssq[:, 64:128], ssq[:, 64:128], AF.Ln,
                 reads=[kk_], writes=[kk_])
            st.s("activation", ssq[:, 64:128], ssq[:, 64:128], AF.Exp,
                 scale=-0.5, reads=[kk_], writes=[kk_])
            st.b.op("gpsimd", lambda e: e.tensor_tensor(
                knb[:, :].rearrange("p (h e) -> p h e", e=64),
                kf[:, :].rearrange("p (h e) -> p h e", e=64),
                bcast(ssq[:, 64:128], 64, 64), ALU.mult),
                reads=[("kf",), kk_], writes=[("knb",)])
            ck = b.take()
            b.interleave(cq, ck)
            for t in range(SLAB):
                emit_tp8(st, qnb[:, t * 1024:(t + 1) * 1024], ("qnb",),
                         qfa[:, :].rearrange("p (c t) -> p c t", c=8)
                         [:, :, (t0 + t) * 128:(t0 + t + 1) * 128],
                         [("qfa", s % SPS)], pt2, identb)
            if t0 == 0:
                st.b.op("gpsimd", lambda e: e.memset(ssf[:, :], 0.0),
                        reads=[("ssf",)], writes=[("ssf",)])
            for h in range(H):
                par = h % 2
                hc = h // 2
                rl = par * 64
                for t in range(SLAB):
                    st.mm(s0[rl:rl + 64, hc * 64:(hc + 1) * 64],
                          knb[:, t * 1024 + h * 64:t * 1024 + (h + 1) * 64],
                          vbb[:, t * 1024 + h * 64:t * 1024 + (h + 1) * 64],
                          t == 0, t == SLAB - 1,
                          reads=[("knb",), ("vbb",)], writes=[skey])
            st.v("tensor_add", ssf[:, :], ssf[:, :], s0[:, :],
                 reads=[skey, ("ssf",)], writes=[("ssf",)])
            if (t0 + SLAB) == TPS:
                for h in range(H):
                    par = h % 2
                    hc = h // 2
                    rl = par * 64
                    st.v("tensor_scalar",
                         ssc[rl:rl + 64, hc * 64:(hc + 1) * 64],
                         ssf[rl:rl + 64, hc * 64:(hc + 1) * 64],
                         scb[rl:rl + 64, h:h + 1], None, ALU.mult,
                         reads=[("ssf",), "scb"], writes=[("ssc", bqi)])

        def p2(g):
            bqi, tt = divmod(g, TPS)
            row0 = bqi * L + tt * 128
            for h in range(H):
                par = h % 2
                hc = h // 2
                rl = par * 64
                st.mm(pb[rl:rl + 64, hc * 128:(hc + 1) * 128],
                      ssc[rl:rl + 64, hc * 64:(hc + 1) * 64],
                      qfa[rl:rl + 64,
                          hc * 2048 + tt * 128:hc * 2048 + (tt + 1) * 128],
                      True, True,
                      reads=[("ssc", bqi), ("qfa", "all")],
                      writes=pk("pb", hc * 128, (hc + 1) * 128))
            st.v("tensor_copy", afm[:, :], pb[:, :],
                 reads=pk("pb", 0, 1024), writes=[("afm",)])
            for half in range(2):
                for c in range(DC):
                    st.mm(pa[:, half * 512:(half + 1) * 512],
                          afm[:, c * 128:(c + 1) * 128],
                          woa[:, c * 1024 + half * 512:
                              c * 1024 + (half + 1) * 512],
                          c == 0, False,
                          reads=[("afm",), "woa"],
                          writes=pk("pa", half * 512, (half + 1) * 512))
                st.mm(pa[:, half * 512:(half + 1) * 512], obf[0:1, 0:128],
                      bsx[0:1, 3072 + half * 512:3072 + (half + 1) * 512],
                      False, True,
                      reads=["obf", "bsx"],
                      writes=pk("pa", half * 512, (half + 1) * 512))
            st.dma("sync", xrs[:, :], xi[row0:row0 + 128, :],
                   writes=[("xrs",)])
            st.v("tensor_add", xrs[:, :], xrs[:, :], pa[:, :],
                 reads=[("xrs",)] + pk("pa", 0, 1024), writes=[("xrs",)])
            st.dma("scalar", xo[row0:row0 + 128, :], xrs[:, :],
                   reads=[("xrs",)])

        for bqi in range(SEQ):
            s0_ = bqi * SPS
            p1a(s0_)
            for s in range(s0_, s0_ + SPS):
                p1b(s)
                b.record()
                p1c(s)
                buf1 = b.take()
                b.record()
                if s + 1 < s0_ + SPS:
                    p1a(s + 1)
                buf2 = b.take()
                b.interleave(buf1, buf2)
            for tt in range(TPS):
                p2(bqi * TPS + tt)


def emit_stage_d(st, xi, xo, wlt, blin, gwt, la, ra, consts):
    nc, b = st.nc, st.b
    identb, idf, triu, ones1f, zbf, obf = consts
    SLAB = 4
    NSL = NT // SLAB
    with ExitStack() as stk:
        st.stk = stk
        wls = st.sb("wls", [128, 8192], BF16)
        bls = st.sb("bls", [1, 1024], BF16)
        gws = st.sb("gws", [128, 64], BF16)
        las = st.sb("las", [128, 256], BF16)
        ras = st.sb("ras", [32, 1024], BF16)
        xt = st.sb("xt", [128, 4096], F32)
        xrs = st.sb("xrs", [128, 2048], F32)
        stt = st.sb("stt", [128, 40], F32)
        sqln = st.sb("sqln", [128, 4096], BF16)
        xnb = st.sb("xnb", [128, 4096], BF16)
        xnf = st.sb("xnf", [128, 2 * 4096], BF16)
        lgt = st.sb("lgt", [128, 32], F32)
        gsc = st.sb("gsc", [128, 160], F32)
        cw = st.sb("cw", [128, 32], F32)
        tsb = st.sb("tsb", [128, 128], BF16)
        tfm = st.sb("tfm", [32, 512], BF16)
        pa = st.ps("pa", [128, 1024])
        pb = st.ps("pb", [128, 1024])
        psm = st.ps("psm", [128, 512])
        pt = st.ps("pt", [128, 1024], BF16)
        pt2 = st.ps("pt2", [128, 1024], BF16)

        st.dma("gpsimd", wls[:, :].rearrange("p (c n) -> p c n", c=8),
               wlt.rearrange("(c p) n -> p c n", p=128), writes=["wls"])
        st.dma("gpsimd", bls[:, :], blin[:, :], writes=["bls"])
        st.dma("gpsimd", gws[:, :].rearrange("p (c n) -> p c n", c=8),
               gwt.rearrange("(c p) n -> p c n", p=128), writes=["gws"])
        st.dma("gpsimd", las[:, :].rearrange("p (c n) -> p c n", c=8),
               la.rearrange("(c p) n -> p c n", p=128), writes=["las"])
        st.dma("gpsimd", ras[:, :], ra[:, :], writes=["ras"])

        def p1a(s):
            t00 = s * SLAB
            row0 = t00 * 128
            xk = ("xt",)
            xsl = xt[:, :]
            st.dma("sync", xsl.rearrange("p (t d) -> p t d", t=4),
                   xi[row0:row0 + 512, :].rearrange("(t p) d -> p t d", p=128),
                   writes=[xk])
            stk_ = ("stt",)
            st.v("reduce_sum", stt[:, 0:4],
                 xsl.rearrange("p (t d) -> p t d", d=1024), axis=AX.X,
                 reads=[xk], writes=[stk_])
            st.s("activation", sqln[:, :], xsl, AF.Square,
                 reads=[xk], writes=[("sqln",)])
            st.v("reduce_sum", stt[:, 4:8],
                 sqln[:, :].rearrange("p (t d) -> p t d", d=1024), axis=AX.X,
                 reads=[("sqln",)], writes=[stk_])
            st.v("tensor_scalar", stt[:, 8:12], stt[:, 0:4],
                 1.0 / D, None, ALU.mult, reads=[stk_], writes=[stk_])
            st.v("tensor_scalar", stt[:, 12:16], stt[:, 4:8],
                 1.0 / D, LN_EPS, ALU.mult, ALU.add, reads=[stk_], writes=[stk_])
            st.s("activation", stt[:, 16:20], stt[:, 8:12],
                 AF.Square, reads=[stk_], writes=[stk_])
            st.v("tensor_sub", stt[:, 20:24], stt[:, 12:16], stt[:, 16:20],
                 reads=[stk_], writes=[stk_])
            st.s("activation", stt[:, 24:28], stt[:, 20:24], AF.Ln,
                 reads=[stk_], writes=[stk_])
            st.s("activation", stt[:, 28:32], stt[:, 24:28], AF.Exp,
                 scale=-0.5, reads=[stk_], writes=[stk_])
            st.v("scalar_tensor_tensor", stt[:, 32:36], stt[:, 8:12], -1.0,
                 stt[:, 28:32], ALU.mult, ALU.mult, reads=[stk_], writes=[stk_])
            for t in range(SLAB):
                st.s("activation", xnb[:, t * 1024:(t + 1) * 1024],
                     xsl[:, t * 1024:(t + 1) * 1024], AF.Identity,
                     bias=stt[:, 32 + t:33 + t], scale=stt[:, 28 + t:29 + t],
                     reads=[xk, stk_], writes=[("xnb",)])
            for t in range(SLAB):
                emit_tp8(st, xnb[:, t * 1024:(t + 1) * 1024], ("xnb",),
                         xnf[:, (s % 2) * 4096 + t * 1024:
                             (s % 2) * 4096 + (t + 1) * 1024]
                         .rearrange("p (c t) -> p c t", c=8),
                         [("xnf", s % 2)], pt, identb)

        def p1b(s):
            rs = s % 2
            xfk = ("xnf", rs)

            def lhs(t, c):
                return xnf[:, rs * 4096 + t * 1024 + c * 128:
                           rs * 4096 + t * 1024 + (c + 1) * 128]

            for t in range(SLAB):
                for c in range(DC):
                    st.mm(psm[:, t * 128:t * 128 + 32], lhs(t, c),
                          las[:, c * 32:(c + 1) * 32], c == 0, c == DC - 1,
                          reads=[xfk, "las"], writes=pk("psm", 0, 512))
                for c in range(DC):
                    st.mm(psm[:, t * 128 + 32:t * 128 + 40], lhs(t, c),
                          gws[:, c * 8:(c + 1) * 8], c == 0, c == DC - 1,
                          reads=[xfk, "gws"], writes=pk("psm", 0, 512))
            st.v("tensor_copy",
                 lgt[:, :].rearrange("p (t e) -> p t e", e=8),
                 psm[:, :].rearrange("p (t x) -> p t x", x=128)[:, :, 32:40],
                 reads=pk("psm", 0, 512), writes=[("lgt",)])
            gk = ("gsc",)
            v3 = lgt[:, :].rearrange("p (t e) -> p t e", e=8)
            st.v("reduce_max", gsc[:, 0:4], v3, axis=AX.X, negate=True,
                 reads=[("lgt",)], writes=[gk])
            st.v("tensor_tensor", gsc[:, 32:64]
                 .rearrange("p (t e) -> p t e", e=8), v3,
                 bcast(gsc[:, 0:4], 4, 8), ALU.add,
                 reads=[("lgt",), gk], writes=[gk])
            st.s("activation", gsc[:, 32:64], gsc[:, 32:64], AF.Exp,
                 reads=[gk], writes=[gk])
            st.v("reduce_sum", gsc[:, 4:8], gsc[:, 32:64]
                 .rearrange("p (t e) -> p t e", e=8), axis=AX.X,
                 reads=[gk], writes=[gk])
            st.v("reciprocal", gsc[:, 8:12], gsc[:, 4:8],
                 reads=[gk], writes=[gk])
            st.v("tensor_tensor", gsc[:, 64:96]
                 .rearrange("p (t e) -> p t e", e=8),
                 gsc[:, 32:64].rearrange("p (t e) -> p t e", e=8),
                 bcast(gsc[:, 8:12], 4, 8), ALU.mult,
                 reads=[gk], writes=[gk])
            st.v("reduce_max", gsc[:, 12:16], gsc[:, 64:96]
                 .rearrange("p (t e) -> p t e", e=8), axis=AX.X,
                 reads=[gk], writes=[gk])
            st.v("tensor_tensor", gsc[:, 96:128]
                 .rearrange("p (t e) -> p t e", e=8),
                 gsc[:, 64:96].rearrange("p (t e) -> p t e", e=8),
                 bcast(gsc[:, 12:16], 4, 8), ALU.is_equal,
                 reads=[gk], writes=[gk])
            st.v("scalar_tensor_tensor", gsc[:, 128:160], gsc[:, 96:128],
                 -1e30, gsc[:, 64:96], ALU.mult, ALU.add,
                 reads=[gk], writes=[gk])
            st.v("reduce_max", gsc[:, 16:20], gsc[:, 128:160]
                 .rearrange("p (t e) -> p t e", e=8), axis=AX.X,
                 reads=[gk], writes=[gk])
            st.v("tensor_tensor", gsc[:, 96:128]
                 .rearrange("p (t e) -> p t e", e=8),
                 gsc[:, 64:96].rearrange("p (t e) -> p t e", e=8),
                 bcast(gsc[:, 16:20], 4, 8), ALU.is_ge,
                 reads=[gk], writes=[gk])
            st.v("tensor_mul", cw[:, :], gsc[:, 64:96], gsc[:, 96:128],
                 reads=[gk], writes=[("cw",)])
            for t in range(SLAB):
                st.v("tensor_tensor",
                     tsb[:, t * 32:(t + 1) * 32]
                     .rearrange("p (e r) -> p e r", r=4),
                     psm[:, t * 128:t * 128 + 32]
                     .rearrange("p (e r) -> p e r", r=4),
                     bcast(cw[:, t * 8:(t + 1) * 8], 8, 4), ALU.mult,
                     reads=pk("psm", 0, 512) + [("cw",)], writes=[("tsb",)])
            for t in range(SLAB):
                st.tr(pt2[0:32, t * 128:(t + 1) * 128],
                      tsb[:, t * 32:(t + 1) * 32], identb[:, :],
                      reads=[("tsb",), "identb"], writes=[("PPT2", 0)])
                st.v("tensor_copy", tfm[0:32, t * 128:(t + 1) * 128],
                     pt2[0:32, t * 128:(t + 1) * 128],
                     reads=[("PPT2", 0)], writes=[("tfm",)])
            for t in range(SLAB):
                pbase = pa if t % 2 == 0 else pb
                pnm = "pa" if t % 2 == 0 else "pb"
                for half in range(2):
                    for c in range(DC):
                        st.mm(pbase[:, half * 512:(half + 1) * 512], lhs(t, c),
                              wls[:, c * 1024 + half * 512:
                                  c * 1024 + (half + 1) * 512],
                              c == 0, False,
                              reads=[xfk, "wls"],
                              writes=pk(pnm, half * 512, (half + 1) * 512))
                    st.mm(pbase[:, half * 512:(half + 1) * 512],
                          obf[0:1, 0:128],
                          bls[0:1, half * 512:(half + 1) * 512], False, False,
                          reads=["obf", "bls"],
                          writes=pk(pnm, half * 512, (half + 1) * 512))
                    st.mm(pbase[:, half * 512:(half + 1) * 512],
                          tfm[0:32, t * 128:(t + 1) * 128],
                          ras[0:32, half * 512:(half + 1) * 512], False, True,
                          reads=[("tfm",), "ras"],
                          writes=pk(pnm, half * 512, (half + 1) * 512))
                g = s * SLAB + t
                row0 = g * 128
                ra_ = t % 2
                xra = xrs[:, ra_ * 1024:(ra_ + 1) * 1024]
                st.dma("sync", xra, xi[row0:row0 + 128, :],
                       writes=[("xrs", ra_)])
                st.v("tensor_add", xra, xra, pbase[:, :],
                     reads=[("xrs", ra_)] + pk(pnm, 0, 1024),
                     writes=[("xrs", ra_)])
                st.dma("scalar", xo[row0:row0 + 128, :], xra,
                       reads=[("xrs", ra_)])

        p1a(0)
        for s in range(NSL):
            b.record()
            p1b(s)
            buf1 = b.take()
            b.record()
            if s + 1 < NSL:
                p1a(s + 1)
            buf2 = b.take()
            b.interleave(buf1, buf2)


def _mk_consts(nc):
    identb = nc.dram_tensor("identb", [128, 128], BF16,
                            kind="ExternalInput").ap()
    idf = nc.dram_tensor("idf", [128, 128], F32, kind="ExternalInput").ap()
    triu = nc.dram_tensor("triu", [128, 128], F32, kind="ExternalInput").ap()
    ones1f = nc.dram_tensor("ones1f", [1, 128], F32,
                            kind="ExternalInput").ap()
    zbf = nc.dram_tensor("zbf", [1, 128], BF16, kind="ExternalInput").ap()
    obf = nc.dram_tensor("obf", [1, 512], BF16, kind="ExternalInput").ap()
    return identb, idf, triu, ones1f, zbf, obf


def _load_consts(st, consts):
    identb, idf, triu, ones1f, zbf, obf = consts
    sb_identb = st.sb("c_identb", [128, 128], BF16)
    sb_idf = st.sb("c_idf", [128, 128], F32)
    sb_triu = st.sb("c_triu", [128, 128], F32)
    sb_ones1f = st.sb("c_ones1f", [1, 128], F32)
    sb_zbf = st.sb("c_zbf", [1, 128], BF16)
    sb_obf = st.sb("c_obf", [1, 512], BF16)
    st.dma("sync", sb_identb[:, :], identb, writes=["identb"])
    st.dma("sync", sb_idf[:, :], idf, writes=["idf"])
    st.dma("sync", sb_triu[:, :], triu, writes=["triu"])
    st.dma("sync", sb_ones1f[:, :], ones1f, writes=["ones1f"])
    st.dma("sync", sb_zbf[:, :], zbf, writes=["zbf"])
    st.dma("sync", sb_obf[:, :], obf, writes=["obf"])
    return (sb_identb, sb_idf, sb_triu, sb_ones1f, sb_zbf, sb_obf)


def _const_arrays():
    return {
        "identb": np.eye(128, dtype=BFNP),
        "idf": np.eye(128, dtype=np.float32),
        "triu": np.triu(np.ones((128, 128), dtype=np.float32)),
        "ones1f": np.ones((1, 128), dtype=np.float32),
        "zbf": np.zeros((1, 128), dtype=BFNP),
        "obf": np.ones((1, 512), dtype=BFNP),
    }


def _din(nc, name, shape):
    return nc.dram_tensor(name, list(shape), F32, kind="ExternalInput").ap()


def _build_stage_program(stage, shapes):
    nc = bass.Bass("TRN2", target_bir_lowering=False)
    consts_d = _mk_consts(nc)
    xi = _din(nc, "xi", [T, D])
    xo = nc.dram_tensor("xo", [T, D], F32, kind="ExternalOutput").ap()
    bld = Builder()
    st = St(nc, bld)
    with ExitStack() as cstk:
        st.stk = cstk
        consts = _load_consts(st, consts_d)
        if stage == "A":
            wq = _din(nc, "a_wq", [D, D])
            wk = _din(nc, "a_wk", [D, D])
            wv = _din(nc, "a_wv", [D, D])
            wo = _din(nc, "a_wo", [D, D])
            wb = _din(nc, "a_wb", [D, H])
            bb = _din(nc, "a_bb", [1, H])
            emit_stage_a(st, xi, xo, wq, wk, wv, wo, wb, bb, consts)
        elif stage == "B":
            w1t = _din(nc, "b_w1t", [NE, D, INTER])
            w2t = _din(nc, "b_w2t", [NE, INTER, D])
            w3t = _din(nc, "b_w3t", [NE, D, INTER])
            sw1t = _din(nc, "b_sw1t", [D, INTER])
            sw2t = _din(nc, "b_sw2t", [INTER, D])
            sw3t = _din(nc, "b_sw3t", [D, INTER])
            gwt = _din(nc, "b_gwt", [D, NE])
            gb = _din(nc, "b_gb", [1, NE])
            emit_stage_b(st, xi, xo, w1t, w2t, w3t, sw1t, sw2t, sw3t,
                         gwt, gb, consts)
        elif stage == "C":
            wq = _din(nc, "c_wq", [D, D])
            wk = _din(nc, "c_wk", [D, D])
            wv = _din(nc, "c_wv", [D, D])
            wo = _din(nc, "c_wo", [D, D])
            bq = _din(nc, "c_bq", [1, D])
            bk = _din(nc, "c_bk", [1, D])
            bv = _din(nc, "c_bv", [1, D])
            bo = _din(nc, "c_bo", [1, D])
            m_in = _din(nc, "c_m", [1, H])
            emit_stage_c(st, xi, xo, wq, wk, wv, wo, bq, bk, bv, bo,
                         m_in, consts)
        elif stage == "D":
            wlt = _din(nc, "d_wlt", [D, D])
            blin = _din(nc, "d_blin", [1, D])
            gwt = _din(nc, "d_gwt", [D, NE])
            la = _din(nc, "d_la", [D, 32])
            ra = _din(nc, "d_ra", [32, D])
            emit_stage_d(st, xi, xo, wlt, blin, gwt, la, ra, consts)
        bld.finalize(nc)
    return nc


def _build_fused_program():
    nc = bass.Bass("TRN2", target_bir_lowering=False)
    consts_d = _mk_consts(nc)
    xi = _din(nc, "xi", [T, D])
    xm1 = nc.dram_tensor("xm1", [T, D], F32).ap()
    xm2 = nc.dram_tensor("xm2", [T, D], F32).ap()
    xm3 = nc.dram_tensor("xm3", [T, D], F32).ap()
    xo = nc.dram_tensor("xo", [T, D], F32, kind="ExternalOutput").ap()
    a_wq = _din(nc, "a_wq", [D, D])
    a_wk = _din(nc, "a_wk", [D, D])
    a_wv = _din(nc, "a_wv", [D, D])
    a_wo = _din(nc, "a_wo", [D, D])
    a_wb = _din(nc, "a_wb", [D, H])
    a_bb = _din(nc, "a_bb", [1, H])
    b_w1t = _din(nc, "b_w1t", [NE, D, INTER])
    b_w2t = _din(nc, "b_w2t", [NE, INTER, D])
    b_w3t = _din(nc, "b_w3t", [NE, D, INTER])
    b_sw1t = _din(nc, "b_sw1t", [D, INTER])
    b_sw2t = _din(nc, "b_sw2t", [INTER, D])
    b_sw3t = _din(nc, "b_sw3t", [D, INTER])
    b_gwt = _din(nc, "b_gwt", [D, NE])
    b_gb = _din(nc, "b_gb", [1, NE])
    c_wq = _din(nc, "c_wq", [D, D])
    c_wk = _din(nc, "c_wk", [D, D])
    c_wv = _din(nc, "c_wv", [D, D])
    c_wo = _din(nc, "c_wo", [D, D])
    c_bq = _din(nc, "c_bq", [1, D])
    c_bk = _din(nc, "c_bk", [1, D])
    c_bv = _din(nc, "c_bv", [1, D])
    c_bo = _din(nc, "c_bo", [1, D])
    c_m = _din(nc, "c_m", [1, H])
    d_wlt = _din(nc, "d_wlt", [D, D])
    d_blin = _din(nc, "d_blin", [1, D])
    d_gwt = _din(nc, "d_gwt", [D, NE])
    d_la = _din(nc, "d_la", [D, 32])
    d_ra = _din(nc, "d_ra", [32, D])
    bld = Builder()
    st = St(nc, bld)
    with ExitStack() as cstk:
        st.stk = cstk
        consts = _load_consts(st, consts_d)
        st.stk = None
        st.prefix = "sa_"
        emit_stage_a(st, xi, xm1, a_wq, a_wk, a_wv, a_wo, a_wb, a_bb, consts)
        bld.barrier()
        st.prefix = "sb_"
        emit_stage_b(st, xm1, xm2, b_w1t, b_w2t, b_w3t, b_sw1t, b_sw2t,
                     b_sw3t, b_gwt, b_gb, consts)
        bld.barrier()
        st.prefix = "sc_"
        emit_stage_c(st, xm2, xm3, c_wq, c_wk, c_wv, c_wo, c_bq, c_bk,
                     c_bv, c_bo, c_m, consts)
        bld.barrier()
        st.prefix = "sd_"
        emit_stage_d(st, xm3, xo, d_wlt, d_blin, d_gwt, d_la, d_ra, consts)
        st.stk = cstk
        bld.finalize(nc)
    return nc


_PROG_CACHE = {}


def _get_prog(stage):
    if stage not in _PROG_CACHE:
        if stage == "FUSED":
            _PROG_CACHE[stage] = _build_fused_program()
        else:
            _PROG_CACHE[stage] = _build_stage_program(stage, None)
    return _PROG_CACHE[stage]


def _host_weights(inputs):
    f = np.ascontiguousarray
    w = {}
    w["a_wq"] = f(inputs["L0_attn_Wq"])
    w["a_wk"] = f(inputs["L0_attn_Wk"])
    w["a_wv"] = f(inputs["L0_attn_Wv"])
    w["a_wo"] = f(inputs["L0_attn_Wo"])
    w["a_wb"] = f(inputs["L0_attn_Wb"])
    w["a_bb"] = f(inputs["L0_attn_bb"].reshape(1, H))
    w["b_w1t"] = f(inputs["L0_ffn_w1"].transpose(0, 2, 1))
    w["b_w2t"] = f(inputs["L0_ffn_w2"].transpose(0, 2, 1))
    w["b_w3t"] = f(inputs["L0_ffn_w3"].transpose(0, 2, 1))
    w["b_sw1t"] = f(inputs["L0_ffn_sw1"].T)
    w["b_sw2t"] = f(inputs["L0_ffn_sw2"].T)
    w["b_sw3t"] = f(inputs["L0_ffn_sw3"].T)
    w["b_gwt"] = f(inputs["L0_ffn_gate_w"].T)
    w["b_gb"] = f(inputs["L0_ffn_gate_bias"].reshape(1, NE))
    w["c_wq"] = f(inputs["L1_attn_Wq"])
    w["c_wk"] = f(inputs["L1_attn_Wk"])
    w["c_wv"] = f(inputs["L1_attn_Wv"])
    w["c_wo"] = f(inputs["L1_attn_Wo"])
    w["c_bq"] = f(inputs["L1_attn_bq"].reshape(1, D))
    w["c_bk"] = f(inputs["L1_attn_bk"].reshape(1, D))
    w["c_bv"] = f(inputs["L1_attn_bv"].reshape(1, D))
    w["c_bo"] = f(inputs["L1_attn_bo"].reshape(1, D))
    w["c_m"] = f(inputs["L1_attn_m"].reshape(1, H))
    w["d_wlt"] = f(inputs["L1_ffn_W_lin"].T)
    w["d_blin"] = f(inputs["L1_ffn_b_lin"].reshape(1, D))
    w["d_gwt"] = f(inputs["L1_ffn_gate_w"].T)
    w["d_la"] = f(inputs["L1_ffn_left"].transpose(1, 0, 2).reshape(D, 32))
    w["d_ra"] = f(inputs["L1_ffn_right"].reshape(32, D))
    return w


_STAGE_KEYS = {
    "A": ["a_wq", "a_wk", "a_wv", "a_wo", "a_wb", "a_bb"],
    "B": ["b_w1t", "b_w2t", "b_w3t", "b_sw1t", "b_sw2t", "b_sw3t",
          "b_gwt", "b_gb"],
    "C": ["c_wq", "c_wk", "c_wv", "c_wo", "c_bq", "c_bk", "c_bv", "c_bo",
          "c_m"],
    "D": ["d_wlt", "d_blin", "d_gwt", "d_la", "d_ra"],
}


def run_stages(x_cores, weights, stages=("A", "B", "C", "D")):
    consts = _const_arrays()
    for stage in stages:
        nc = _get_prog(stage)
        in_maps = []
        for ci in range(NCORES):
            m = {"xi": x_cores[ci]}
            m.update(consts)
            for k in _STAGE_KEYS[stage]:
                m[k] = weights[k]
            in_maps.append(m)
        res = run_bass_kernel_spmd(nc, in_maps, core_ids=list(range(NCORES)))
        x_cores = [res.results[ci]["xo"] for ci in range(NCORES)]
    return x_cores


def run_fused(x_cores, weights):
    consts = _const_arrays()
    nc = _get_prog("FUSED")
    in_maps = []
    for ci in range(NCORES):
        m = {"xi": x_cores[ci]}
        m.update(consts)
        for stage in ("A", "B", "C", "D"):
            for k in _STAGE_KEYS[stage]:
                m[k] = weights[k]
        in_maps.append(m)
    res = run_bass_kernel_spmd(nc, in_maps, core_ids=list(range(NCORES)))
    return [res.results[ci]["xo"] for ci in range(NCORES)]


def kernel(**inputs):
    x = np.ascontiguousarray(inputs["L0_attn_x"].astype(np.float32,
                                                        copy=False))
    xc = [np.ascontiguousarray(x[2 * c:2 * c + 2].reshape(T, D))
          for c in range(NCORES)]
    weights = _host_weights(inputs)
    xc = run_fused(xc, weights)
    out = np.stack([a.reshape(2, L, D) for a in xc], 0).reshape(16, L, D)
    return out.astype(np.float32, copy=False)


# version 12
# speedup vs baseline: 1.2125x; 1.2125x over previous
import math
from contextlib import ExitStack

import numpy as np
import ml_dtypes

import concourse.bass as bass
import concourse.mybir as mybir
from concourse.bass_utils import run_bass_kernel_spmd

F32 = mybir.dt.float32
BF16 = mybir.dt.bfloat16
AF = mybir.ActivationFunctionType
ALU = mybir.AluOpType
AX = mybir.AxisListType
BFNP = ml_dtypes.bfloat16

NCORES = 8
SEQ = 2
L = 2048
D = 1024
DC = 8
H = 16
DH = 64
TPS = 16
T = SEQ * L
NT = T // 128
INTER = 2048
NE = 8
GT = 8
NG = NT // GT
LN_EPS = 1e-5

ENGINES = ("tensor", "vector", "scalar", "gpsimd", "sync")

MERGED_MASK = True
MERGED_VSB = True
POOL_NORM = True
PIPELINED_A = True
A_TILES = None
A_CUT = 9


class Builder:

    def __init__(self):
        self.ops = []
        self.last_w = {}
        self.readers = {}
        self.need_inc = set()
        self.dma_chan = {}
        self.rec = []

    def record(self):
        self.rec.append([])

    def take(self):
        return self.rec.pop()

    def interleave(self, *bufs):
        items = []
        for b in bufs:
            n = len(b)
            for i, it in enumerate(b):
                items.append(((i + 0.5) / n, it))
        items.sort(key=lambda x: x[0])
        for _, (engine, fn, reads, writes, dma) in items:
            self.op(engine, fn, reads=reads, writes=writes, dma=dma)

    def op(self, engine, fn, reads=(), writes=(), dma=False):
        if self.rec:
            self.rec[-1].append((engine, fn, tuple(reads), tuple(writes), dma))
            return -1
        i = len(self.ops)
        chan = None
        if dma:
            key = writes[0] if writes else (reads[0] if reads else "misc")
            chan = self.dma_chan.setdefault(key, len(self.dma_chan))
        deps = set()
        for k in reads:
            lw = self.last_w.get(k)
            if lw is not None:
                deps.add(lw)
        for k in writes:
            lw = self.last_w.get(k)
            if lw is not None:
                deps.add(lw)
            for r in self.readers.get(k, ()):
                deps.add(r)
        deps.discard(i)
        kept = set()
        for d in deps:
            od = self.ops[d]
            if (engine == "tensor" and od["engine"] == "tensor"
                    and not dma and not od["dma"]):
                continue
            kept.add(d)
            self.need_inc.add(d)
        for k in reads:
            self.readers.setdefault(k, []).append(i)
        for k in writes:
            self.last_w[k] = i
            self.readers[k] = []
        self.ops.append(dict(engine=engine, fn=fn, deps=kept, dma=dma,
                             chan=chan, kind="op"))
        return i

    def barrier(self):
        seen = set()
        for i in range(len(self.ops) - 1, -1, -1):
            o = self.ops[i]
            if o["kind"] != "op":
                break
            s = ("D%d" % o["chan"]) if o["dma"] else o["engine"]
            if s not in seen:
                seen.add(s)
                self.need_inc.add(i)
        self.ops.append(dict(engine=None, fn=None, deps=None, dma=False,
                             kind="barrier"))

    def finalize(self, nc):
        self.barrier()
        streams = list(ENGINES) + ["D%d" % c for c in range(len(self.dma_chan))]
        sidx = {s: j for j, s in enumerate(streams)}
        cnt = {s: 0 for s in streams}
        inc_no = {}
        bar_tot = {}
        for i, o in enumerate(self.ops):
            if o["kind"] == "barrier":
                bar_tot[i] = dict(cnt)
                continue
            s = ("D%d" % o["chan"]) if o["dma"] else o["engine"]
            if o["dma"]:
                cnt[s] += 16
                inc_no[i] = (s, cnt[s])
            elif i in self.need_inc:
                cnt[s] += 1
                inc_no[i] = (s, cnt[s])
        nS = len(streams)
        evc = {e: [0] * nS for e in ENGINES}
        ccq = {s: [0] * nS for s in streams}
        opvc = {}
        sched = {e: [] for e in ENGINES}
        for i, o in enumerate(self.ops):
            if o["kind"] == "barrier":
                tot = bar_tot[i]
                for e in ENGINES:
                    waits = []
                    for s in streams:
                        if tot[s] > evc[e][sidx[s]]:
                            waits.append((s, tot[s]))
                    sched[e].append((waits, None))
                    vc = evc[e]
                    for s in streams:
                        if tot[s] > vc[sidx[s]]:
                            vc[sidx[s]] = tot[s]
                continue
            e = o["engine"]
            vc = evc[e]
            need = {}
            for d in o["deps"]:
                s, v = inc_no[d]
                j = sidx[s]
                if vc[j] < v and need.get(s, 0) < v:
                    need[s] = v
            waits = sorted(need.items())
            for d in o["deps"]:
                dv = opvc.get(d)
                if dv is not None:
                    for j in range(nS):
                        if dv[j] > vc[j]:
                            vc[j] = dv[j]
            mys = ("D%d" % o["chan"]) if o["dma"] else e
            mine = inc_no.get(i)
            if mine is not None:
                myvc = [max(a, b) for a, b in zip(vc, ccq[mys])]
                myvc[sidx[mine[0]]] = max(myvc[sidx[mine[0]]], mine[1])
                opvc[i] = myvc
                ccq[mys] = [max(a, b) for a, b in zip(ccq[mys], myvc)]
            sched[e].append((waits, i))
        with ExitStack() as stk:
            sems = {s: stk.enter_context(nc.semaphore("sem_" + s))
                    for s in streams
                    if cnt[s] > 0 or any(w[0] == s
                                         for e in ENGINES
                                         for ws, _ in sched[e] for w in ws)}
            block = stk.enter_context(nc.Block())

            def mk_body(e):
                def body(eng):
                    for waits, i in sched[e]:
                        for s, v in waits:
                            eng.wait_ge(sems[s], v)
                        if i is None:
                            continue
                        o = self.ops[i]
                        inst = o["fn"](eng)
                        mine = inc_no.get(i)
                        if mine is not None:
                            inst.then_inc(sems[mine[0]], 16 if o["dma"] else 1)
                return body

            for e in ENGINES:
                getattr(block, e)(mk_body(e))


class St:

    def __init__(self, nc, bld):
        self.nc = nc
        self.b = bld
        self.stk = None
        self.t = {}

    prefix = ""

    def sb(self, name, shape, dtype):
        nm = self.prefix + name
        self.t[nm] = self.stk.enter_context(
            self.nc.sbuf_tensor(nm, shape, dtype))
        return self.t[nm]

    def ps(self, name, shape, dtype=F32):
        nm = self.prefix + name
        self.t[nm] = self.stk.enter_context(
            self.nc.psum_tensor(nm, shape, dtype))
        return self.t[nm]

    def dma(self, q, out, in_, reads=(), writes=()):
        self.b.op(q, lambda e: e.dma_start(out=out, in_=in_),
                  reads=reads, writes=writes, dma=True)

    def mm(self, out, lhsT, rhs, start, stop, reads=(), writes=()):
        self.b.op("tensor",
                  lambda e: e.matmul(out, lhsT, rhs, start=start, stop=stop),
                  reads=reads, writes=writes)

    def tr(self, out, in_, ident, reads=(), writes=()):
        self.b.op("tensor", lambda e: e.transpose(out, in_, ident),
                  reads=reads, writes=writes)

    def v(self, meth, *a, reads=(), writes=(), **kw):
        self.b.op("vector", lambda e: getattr(e, meth)(*a, **kw),
                  reads=reads, writes=writes)

    def s(self, meth, *a, reads=(), writes=(), **kw):
        self.b.op("scalar", lambda e: getattr(e, meth)(*a, **kw),
                  reads=reads, writes=writes)


def pk(name, lo, hi, itemsize=4):
    per_bank = 2048 // itemsize
    return [("P" + name, c) for c in range(lo // per_bank,
                                           (hi - 1) // per_bank + 1)]


def emit_ln(st, xt_ap, xkeys, stt, stkey, sq_ap, sqkey, xnb_ap, xnkey):
    st.v("reduce_sum", stt[:, 0:1], xt_ap, axis=AX.X,
         reads=xkeys, writes=[stkey])
    st.s("activation", sq_ap, xt_ap, AF.Square, reads=xkeys, writes=[sqkey])
    st.v("reduce_sum", stt[:, 1:2], sq_ap, axis=AX.X,
         reads=[sqkey], writes=[stkey])
    st.v("tensor_scalar", stt[:, 2:3], stt[:, 0:1], 1.0 / D, None, ALU.mult,
         reads=[stkey], writes=[stkey])
    st.v("tensor_scalar", stt[:, 3:4], stt[:, 1:2], 1.0 / D, LN_EPS,
         ALU.mult, ALU.add, reads=[stkey], writes=[stkey])
    st.s("activation", stt[:, 4:5], stt[:, 2:3], AF.Square,
         reads=[stkey], writes=[stkey])
    st.v("tensor_sub", stt[:, 5:6], stt[:, 3:4], stt[:, 4:5],
         reads=[stkey], writes=[stkey])
    st.s("activation", stt[:, 6:7], stt[:, 5:6], AF.Ln,
         reads=[stkey], writes=[stkey])
    st.s("activation", stt[:, 7:8], stt[:, 6:7], AF.Exp, scale=-0.5,
         reads=[stkey], writes=[stkey])
    st.v("tensor_scalar", stt[:, 8:9], stt[:, 2:3], stt[:, 7:8], -1.0,
         ALU.mult, ALU.mult, reads=[stkey], writes=[stkey])
    st.s("activation", xnb_ap, xt_ap, AF.Identity,
         bias=stt[:, 8:9], scale=stt[:, 7:8],
         reads=list(xkeys) + [stkey], writes=[xnkey])


def emit_tp8(st, src_ap, srckey, dst_view3, dstkeys, pt, identb):
    for c in range(DC):
        st.tr(pt[:, c * 128:(c + 1) * 128], src_ap[:, c * 128:(c + 1) * 128],
              identb[:, :], reads=[srckey, "identb"], writes=[("PPT", 0)])
    st.v("tensor_copy", dst_view3,
         pt[:, :].rearrange("p (c t) -> p c t", c=8),
         reads=[("PPT", 0)], writes=dstkeys)


def emit_softmax_top2(st, lg_ap, lgkeys, gsc, gkey, cmb_ap, cmbkey,
                      bias_ap=None, biaskey=None):
    st.v("reduce_max", gsc[:, 0:1], lg_ap, axis=AX.X, negate=True,
         reads=lgkeys, writes=[gkey])
    st.s("activation", gsc[:, 8:16], lg_ap, AF.Exp, bias=gsc[:, 0:1],
         reads=list(lgkeys) + [gkey], writes=[gkey])
    st.v("reduce_sum", gsc[:, 1:2], gsc[:, 8:16], axis=AX.X,
         reads=[gkey], writes=[gkey])
    st.v("reciprocal", gsc[:, 2:3], gsc[:, 1:2], reads=[gkey], writes=[gkey])
    st.v("tensor_scalar", gsc[:, 16:24], gsc[:, 8:16], gsc[:, 2:3], None,
         ALU.mult, reads=[gkey], writes=[gkey])
    if bias_ap is not None:
        st.v("tensor_add", gsc[:, 24:32], gsc[:, 16:24], bias_ap,
             reads=[gkey, biaskey], writes=[gkey])
        bsl = gsc[:, 24:32]
    else:
        bsl = gsc[:, 16:24]
    st.v("reduce_max", gsc[:, 3:4], bsl, axis=AX.X, reads=[gkey], writes=[gkey])
    st.v("tensor_scalar", gsc[:, 32:40], bsl, gsc[:, 3:4], None, ALU.is_equal,
         reads=[gkey], writes=[gkey])
    st.v("scalar_tensor_tensor", gsc[:, 40:48], gsc[:, 32:40], -1e30, bsl,
         ALU.mult, ALU.add, reads=[gkey], writes=[gkey])
    st.v("reduce_max", gsc[:, 4:5], gsc[:, 40:48], axis=AX.X,
         reads=[gkey], writes=[gkey])
    st.v("tensor_scalar", gsc[:, 48:56], bsl, gsc[:, 4:5], None, ALU.is_ge,
         reads=[gkey], writes=[gkey])
    st.v("tensor_mul", cmb_ap, gsc[:, 16:24], gsc[:, 48:56],
         reads=[gkey], writes=[cmbkey])


def bcast(ap2, h, rep):
    import copy
    v = ap2.rearrange("p (h o) -> p h o", o=1)
    v = copy.deepcopy(v)
    v.ap[-1] = [0, rep]
    return v


def emit_stage_a(st, xi, xo, wq, wk, wv, wo, wb, bb, consts):
    nc, b = st.nc, st.b
    identb, idf, triu, ones1f, zbf, obf = consts
    SLAB = 4
    NSL = NT // SLAB
    with ExitStack() as stk:
        st.stk = stk
        wqa = st.sb("wqa", [128, 8192], BF16)
        wka = st.sb("wka", [128, 8192], BF16)
        wva = st.sb("wva", [128, 8192], BF16)
        woa = st.sb("woa", [128, 8192], BF16)
        wba = st.sb("wba", [128, 128], BF16)
        bbs = st.sb("bbs", [1, 16], F32)
        xt = st.sb("xt", [128, 4096], BF16)
        xrs = st.sb("xrs", [128, 1024], F32)
        stt = st.sb("stt", [128, 128], F32)
        sq = st.sb("sq", [128, 4096], BF16)
        sqk = st.sb("sqk", [128, 4096], BF16)
        sqln = st.sb("sqln", [128, 4096], BF16)
        xnb = st.sb("xnb", [128, 4096], BF16)
        xnf = st.sb("xnf", [128, 2 * 4096], BF16)
        qf = st.sb("qf", [128, 4096], BF16)
        kf = st.sb("kf", [128, 4096], BF16)
        ssq = st.sb("ssq", [128, 256], F32)
        qnb = st.sb("qnb", [128, 4096], BF16)
        knb = st.sb("knb", [128, 4096], BF16)
        vsb = st.sb("vsb", [128, 2 * 4096], BF16)
        qfm = st.sb("qfm", [128, 4096], BF16)
        kfm = st.sb("kfm", [128, 4096], BF16)
        amb = st.sb("amb", [128, 2048], BF16)
        ssb = st.sb("ssb", [128, 1024], BF16)
        ssf = st.sb("ssf", [128, 512], F32)
        afm = st.sb("afm", [128, 1024], BF16)
        pa = st.ps("pa", [128, 1024])
        pb = st.ps("pb", [128, 1024])
        psm = st.ps("psm", [128, 512])
        s0 = st.ps("s0", [128, 512])
        pt = st.ps("pt", [128, 1024], BF16)
        pt2 = st.ps("pt2", [128, 1024], BF16)
        skey = ("Ps0", 0)

        st.dma("gpsimd", wqa[:, :].rearrange("p (c n) -> p c n", c=8),
               wq.rearrange("(c p) n -> p c n", p=128), writes=["wqa"])
        st.dma("gpsimd", wka[:, :].rearrange("p (c n) -> p c n", c=8),
               wk.rearrange("(c p) n -> p c n", p=128), writes=["wka"])
        st.dma("gpsimd", wva[:, :].rearrange("p (c n) -> p c n", c=8),
               wv.rearrange("(c p) n -> p c n", p=128), writes=["wva"])
        st.dma("gpsimd", woa[:, :].rearrange("p (c n) -> p c n", c=8),
               wo.rearrange("(c p) n -> p c n", p=128), writes=["woa"])
        st.dma("gpsimd", wba[:, :].rearrange("p (c n) -> p c n", c=8),
               wb.rearrange("(c p) n -> p c n", p=128), writes=["wba"])
        st.dma("sync", bbs[:, :], bb[:, :], writes=["bbs"])

        def p1a(s):
            r = s % 2
            bq = (s * SLAB) // TPS
            t0 = (s * SLAB) % TPS
            row0 = bq * L + t0 * 128
            xk = ("xt",)
            xsl = xt[:, :]
            st.dma("gpsimd", xsl.rearrange("p (t d) -> p t d", t=4),
                   xi[row0:row0 + 512, :].rearrange("(t p) d -> p t d", p=128),
                   writes=[xk])
            so = r * 64
            stk_ = ("stt", r)
            v3 = xsl.rearrange("p (t d) -> p t d", d=1024)
            st.v("reduce_sum", stt[:, so:so + 4], v3, axis=AX.X,
                 reads=[xk], writes=[stk_])
            st.s("activation", sqln[:, :], xsl, AF.Square,
                 reads=[xk], writes=[("sqln",)])
            st.v("reduce_sum", stt[:, so + 4:so + 8],
                 sqln[:, :].rearrange("p (t d) -> p t d", d=1024), axis=AX.X,
                 reads=[("sqln",)], writes=[stk_])
            st.v("tensor_scalar", stt[:, so + 8:so + 12], stt[:, so:so + 4],
                 1.0 / D, None, ALU.mult, reads=[stk_], writes=[stk_])
            st.v("tensor_scalar", stt[:, so + 12:so + 16],
                 stt[:, so + 4:so + 8], 1.0 / D, LN_EPS, ALU.mult, ALU.add,
                 reads=[stk_], writes=[stk_])
            st.s("activation", stt[:, so + 16:so + 20], stt[:, so + 8:so + 12],
                 AF.Square, reads=[stk_], writes=[stk_])
            st.v("tensor_sub", stt[:, so + 20:so + 24], stt[:, so + 12:so + 16],
                 stt[:, so + 16:so + 20], reads=[stk_], writes=[stk_])
            st.s("activation", stt[:, so + 24:so + 28], stt[:, so + 20:so + 24],
                 AF.Ln, reads=[stk_], writes=[stk_])
            st.s("activation", stt[:, so + 28:so + 32], stt[:, so + 24:so + 28],
                 AF.Exp, scale=-0.5, reads=[stk_], writes=[stk_])
            st.v("scalar_tensor_tensor", stt[:, so + 32:so + 36],
                 stt[:, so + 8:so + 12], -1.0, stt[:, so + 28:so + 32],
                 ALU.mult, ALU.mult, reads=[stk_], writes=[stk_])
            for t in range(SLAB):
                st.s("activation", xnb[:, t * 1024:(t + 1) * 1024],
                     xsl[:, t * 1024:(t + 1) * 1024], AF.Identity,
                     bias=stt[:, so + 32 + t:so + 33 + t],
                     scale=stt[:, so + 28 + t:so + 29 + t],
                     reads=[xk, stk_], writes=[("xnb",)])
            for t in range(SLAB):
                emit_tp8(st, xnb[:, t * 1024:(t + 1) * 1024], ("xnb",),
                         xnf[:, r * 4096 + t * 1024:r * 4096 + (t + 1) * 1024]
                         .rearrange("p (c t) -> p c t", c=8),
                         [("xnf", r)], pt, identb)

        def p1b(s):
            r = s % 2
            xfk = ("xnf", r)

            def lhs(t, c):
                return xnf[:, r * 4096 + t * 1024 + c * 128:
                           r * 4096 + t * 1024 + (c + 1) * 128]

            for t in range(SLAB):
                for c in range(DC):
                    st.mm(pa[:, 0:512], lhs(t, c),
                          wqa[:, c * 1024:c * 1024 + 512],
                          c == 0, c == DC - 1,
                          reads=[xfk, "wqa"], writes=pk("pa", 0, 512))
                    st.mm(pa[:, 512:1024], lhs(t, c),
                          wqa[:, c * 1024 + 512:(c + 1) * 1024],
                          c == 0, c == DC - 1,
                          reads=[xfk, "wqa"], writes=pk("pa", 512, 1024))
                for c in range(DC):
                    st.mm(pb[:, 0:512], lhs(t, c),
                          wka[:, c * 1024:c * 1024 + 512],
                          c == 0, c == DC - 1,
                          reads=[xfk, "wka"], writes=pk("pb", 0, 512))
                    st.mm(pb[:, 512:1024], lhs(t, c),
                          wka[:, c * 1024 + 512:(c + 1) * 1024],
                          c == 0, c == DC - 1,
                          reads=[xfk, "wka"], writes=pk("pb", 512, 1024))
                for c in range(DC):
                    st.mm(psm[:, t * 16:(t + 1) * 16], lhs(t, c),
                          wba[:, c * 16:(c + 1) * 16], c == 0, False,
                          reads=[xfk, "wba"], writes=pk("psm", 0, 64))
                st.mm(psm[:, t * 16:(t + 1) * 16], ones1f[0:1, :], bbs[0:1, :],
                      False, True, reads=["ones1f", "bbs"],
                      writes=pk("psm", 0, 64))
                st.v("tensor_copy", qf[:, t * 1024:(t + 1) * 1024], pa[:, :],
                     reads=pk("pa", 0, 1024), writes=[("qf",)])
                st.s("copy", kf[:, t * 1024:(t + 1) * 1024], pb[:, :],
                     reads=pk("pb", 0, 1024), writes=[("kf",)])
                for c in range(DC):
                    st.mm(pa[:, 0:512], lhs(t, c),
                          wva[:, c * 1024:c * 1024 + 512],
                          c == 0, c == DC - 1,
                          reads=[xfk, "wva"], writes=pk("pa", 0, 512))
                    st.mm(pa[:, 512:1024], lhs(t, c),
                          wva[:, c * 1024 + 512:(c + 1) * 1024],
                          c == 0, c == DC - 1,
                          reads=[xfk, "wva"], writes=pk("pa", 512, 1024))
                st.v("tensor_copy",
                     vsb[:, r * 4096 + t * 1024:r * 4096 + (t + 1) * 1024],
                     pa[:, :],
                     reads=pk("pa", 0, 1024), writes=[("vsb0", r)])

        def p1c(s):
            r = s % 2
            kb_, kq_, kk_ = ("ssq", "b"), ("ssq", "q"), ("ssq", "k")
            b.record()
            st.s("activation", ssq[:, 192:256], psm[:, 0:64],
                 AF.Exp, scale=-1.0, reads=pk("psm", 0, 64), writes=[kb_])
            st.v("tensor_scalar", ssq[:, 192:256], ssq[:, 192:256],
                 1.0, None, ALU.add, reads=[kb_], writes=[kb_])
            st.v("reciprocal", ssq[:, 128:192], ssq[:, 192:256],
                 reads=[kb_], writes=[kb_])
            st.b.op("gpsimd", lambda e, _r=r: e.tensor_tensor(
                vsb[:, _r * 4096:(_r + 1) * 4096]
                .rearrange("p (h e) -> p h e", e=64),
                vsb[:, _r * 4096:(_r + 1) * 4096]
                .rearrange("p (h e) -> p h e", e=64),
                bcast(ssq[:, 128:192], 64, 64), ALU.mult),
                reads=[("vsb0", r), kb_], writes=[("vsb", r)])
            chain_b = b.take()
            b.record()
            st.s("activation", sq[:, :], qf[:, :], AF.Square,
                 reads=[("qf",)], writes=[("sq",)])
            st.v("reduce_sum", ssq[:, 0:64],
                 sq[:, :].rearrange("p (h e) -> p h e", e=64), axis=AX.X,
                 reads=[("sq",)], writes=[kq_])
            st.s("activation", ssq[:, 0:64], ssq[:, 0:64],
                 AF.Ln, reads=[kq_], writes=[kq_])
            st.s("activation", ssq[:, 0:64], ssq[:, 0:64],
                 AF.Exp, scale=-0.5, reads=[kq_], writes=[kq_])
            st.b.op("gpsimd", lambda e: e.tensor_tensor(
                qnb[:, :].rearrange("p (h e) -> p h e", e=64),
                qf[:, :].rearrange("p (h e) -> p h e", e=64),
                bcast(ssq[:, 0:64], 64, 64), ALU.mult),
                reads=[("qf",), kq_], writes=[("qnb",)])
            chain_q = b.take()
            b.record()
            st.s("activation", sqk[:, :], kf[:, :], AF.Square,
                 reads=[("kf",)], writes=[("sqk",)])
            st.v("reduce_sum", ssq[:, 64:128],
                 sqk[:, :].rearrange("p (h e) -> p h e", e=64), axis=AX.X,
                 reads=[("sqk",)], writes=[kk_])
            st.s("activation", ssq[:, 64:128], ssq[:, 64:128],
                 AF.Ln, reads=[kk_], writes=[kk_])
            st.s("activation", ssq[:, 64:128], ssq[:, 64:128],
                 AF.Exp, scale=-0.5, reads=[kk_], writes=[kk_])
            st.b.op("gpsimd", lambda e: e.tensor_tensor(
                knb[:, :].rearrange("p (h e) -> p h e", e=64),
                kf[:, :].rearrange("p (h e) -> p h e", e=64),
                bcast(ssq[:, 64:128], 64, 64), ALU.mult),
                reads=[("kf",), kk_], writes=[("knb",)])
            chain_k = b.take()
            b.interleave(chain_b, chain_q, chain_k)
            for t in range(SLAB):
                emit_tp8(st, qnb[:, t * 1024:(t + 1) * 1024], ("qnb",),
                         qfm[:, t * 1024:(t + 1) * 1024]
                         .rearrange("p (c t) -> p c t", c=8),
                         [("qfm",)], pt2, identb)
            for t in range(SLAB):
                emit_tp8(st, knb[:, t * 1024:(t + 1) * 1024], ("knb",),
                         kfm[:, t * 1024:(t + 1) * 1024]
                         .rearrange("p (c t) -> p c t", c=8),
                         [("kfm",)], pt2, identb)

        def p2(s, t):
            g = s * SLAB + t
            bq, tt = divmod(g, TPS)
            ra = g % 2
            rs = s % 2
            row0 = bq * L + tt * 128
            off = rs * 4096 + t * 1024
            if tt == 0:
                st.b.op("gpsimd", lambda e: e.memset(ssf[:, :], 0.0),
                        reads=[("ssf",)], writes=[("ssf",)])
            for h in [hh for hh in range(H) if hh % 2 == 0] + \
                     [hh for hh in range(H) if hh % 2 == 1]:
                par = h % 2
                hc = h // 2
                rl = par * 64
                dst = pa if h < 8 else pb
                dnm = "pa" if h < 8 else "pb"
                col = (h % 8) * 128
                st.mm(dst[:, col:col + 128],
                      kfm[rl:rl + 64, t * 1024 + hc * 128:
                          t * 1024 + (hc + 1) * 128],
                      qfm[rl:rl + 64, t * 1024 + hc * 128:
                          t * 1024 + (hc + 1) * 128],
                      True, True,
                      reads=[("kfm",), ("qfm",)],
                      writes=pk(dnm, col, col + 128))
            for h in range(H):
                par = h % 2
                hc = h // 2
                rl = par * 64
                st.mm(s0[rl:rl + 64, hc * 64:(hc + 1) * 64],
                      knb[:, t * 1024 + h * 64:t * 1024 + (h + 1) * 64],
                      vsb[:, off + h * 64:off + (h + 1) * 64],
                      True, True,
                      reads=[("knb",), ("vsb", rs)], writes=[skey])
            st.v("tensor_tensor",
                 amb[:, 0:1024].rearrange("p (h l) -> p h l", l=128),
                 pa[:, :].rearrange("p (h l) -> p h l", l=128),
                 bcast_cols(triu[:, :], 8), ALU.mult,
                 reads=pk("pa", 0, 1024) + ["triu"], writes=[("amb", 0)])
            st.v("tensor_tensor",
                 amb[:, 1024:2048].rearrange("p (h l) -> p h l", l=128),
                 pb[:, :].rearrange("p (h l) -> p h l", l=128),
                 bcast_cols(triu[:, :], 8), ALU.mult,
                 reads=pk("pb", 0, 1024) + ["triu"], writes=[("amb", 1)])
            for h in range(H):
                par = h % 2
                hc = h // 2
                rl = par * 64
                okeys = pk("pa", hc * 128, (hc + 1) * 128)
                st.mm(pa[rl:rl + 64, hc * 128:(hc + 1) * 128],
                      vsb[:, off + h * 64:off + (h + 1) * 64],
                      amb[:, (h % 8) * 128 + (0 if h < 8 else 1024):
                          (h % 8) * 128 + 128 + (0 if h < 8 else 1024)],
                      True, tt == 0,
                      reads=[("vsb", rs), ("amb", 0 if h < 8 else 1)],
                      writes=okeys)
                if tt > 0:
                    st.mm(pa[rl:rl + 64, hc * 128:(hc + 1) * 128],
                          ssb[rl:rl + 64,
                              (tt % 2) * 512 + hc * 64:
                              (tt % 2) * 512 + (hc + 1) * 64],
                          qfm[rl:rl + 64, t * 1024 + hc * 128:
                              t * 1024 + (hc + 1) * 128],
                          False, True,
                          reads=[("ssb", tt % 2), ("qfm",)], writes=okeys)
            if tt + 1 < TPS:
                st.v("tensor_add", ssf[:, :], ssf[:, :], s0[:, :],
                     reads=[skey, ("ssf",)], writes=[("ssf",)])
                st.v("tensor_copy",
                     ssb[:, ((tt + 1) % 2) * 512:((tt + 1) % 2 + 1) * 512],
                     ssf[:, :], reads=[("ssf",)],
                     writes=[("ssb", (tt + 1) % 2)])
            st.v("tensor_copy", afm[:, :], pa[:, :],
                 reads=pk("pa", 0, 1024), writes=[("afm",)])
            for c in range(DC):
                st.mm(pb[:, 0:512],
                      afm[:, c * 128:(c + 1) * 128],
                      woa[:, c * 1024:c * 1024 + 512], c == 0, c == DC - 1,
                      reads=[("afm",), "woa"], writes=pk("pb", 0, 512))
                st.mm(pb[:, 512:1024],
                      afm[:, c * 128:(c + 1) * 128],
                      woa[:, c * 1024 + 512:(c + 1) * 1024],
                      c == 0, c == DC - 1,
                      reads=[("afm",), "woa"], writes=pk("pb", 512, 1024))
            xra = xrs[:, :]
            st.dma("sync", xra, xi[row0:row0 + 128, :], writes=[("xrs",)])
            st.v("tensor_add", xra, xra, pb[:, :],
                 reads=[("xrs",)] + pk("pb", 0, 1024), writes=[("xrs",)])
            st.dma("scalar", xo[row0:row0 + 128, :], xra,
                   reads=[("xrs",)])

        NSLT = (A_TILES // SLAB) if A_TILES else NSL
        p1a(0)
        p1b(0)
        p1c(0)
        p1a(1)
        for s in range(NSLT):
            if s + 1 < NSLT:
                p1b(s + 1)
            for t in range(SLAB):
                p2(s, t)
            b.record()
            if s + 1 < NSLT:
                p1c(s + 1)
            buf1 = b.take()
            b.record()
            if s + 2 < NSLT:
                p1a(s + 2)
            buf2 = b.take()
            if buf1 or buf2:
                b.interleave(buf1, buf2)


def bcast_cols(ap2, rep):
    import copy
    v = copy.deepcopy(ap2)
    n = v.ap[-1]
    v.ap = v.ap[:-1] + [[0, rep], n]
    return v


def emit_stage_b(st, xi, xo, w1t, w2t, w3t, sw1t, sw2t, sw3t, gwt, gb, consts,
                 experts=tuple(range(NE + 1))):
    nc, b = st.nc, st.b
    identb, idf, triu, ones1f, zbf, obf = consts
    with ExitStack() as stk:
        st.stk = stk
        w1q = st.sb("w1q", [128, 2 * 4096], BF16)
        w3q = st.sb("w3q", [128, 2 * 4096], BF16)
        w2b = st.sb("w2b", [128, 16384], BF16)
        gws = st.sb("gws", [128, 64], BF16)
        gbs = st.sb("gbs", [1, 8], F32)
        gbx = st.sb("gbx", [128, 8], F32)
        x1s = st.sb("x1s", [128, 8192], F32)
        xtb = st.sb("xtb", [128, 4096], BF16)
        stt = st.sb("stt", [128, 40], F32)
        sqln = st.sb("sqln", [128, 4096], BF16)
        xnb = st.sb("xnb", [128, 4096], BF16)
        xnf = st.sb("xnf", [128, 2 * 8192], BF16)
        lgt = st.sb("lgt", [128, 64], F32)
        gsc = st.sb("gsc", [128, 320], F32)
        cmb = st.sb("cmb", [128, 2 * 64], F32)
        hsl = st.sb("hsl", [128, 1024], F32)
        hb = st.sb("hb", [128, 16384], BF16)
        pa = st.ps("pa", [128, 1024])
        pb = st.ps("pb", [128, 1024])
        psm = st.ps("psm", [128, 512])
        pt = st.ps("pt", [128, 1024], BF16)

        st.dma("gpsimd", gws[:, :].rearrange("p (c n) -> p c n", c=8),
               gwt.rearrange("(c p) n -> p c n", p=128), writes=["gws"])
        st.dma("sync", gbs[:, :], gb[:, :], writes=["gbs"])
        st.mm(psm[:, 0:8], ones1f[0:1, :], gbs[0:1, :], True, True,
              reads=["ones1f", "gbs"], writes=pk("psm", 0, 8))
        st.v("tensor_copy", gbx[:, :], psm[:, 0:8],
             reads=pk("psm", 0, 8), writes=["gbx"])

        def wsrc(e, which):
            if e < NE:
                return {"w1": w1t, "w2": w2t, "w3": w3t}[which][e]
            return {"w1": sw1t, "w2": sw2t, "w3": sw3t}[which]

        def slab_ln(g, sl):
            rg = g % 2
            row0 = g * 1024 + sl * 512
            xsl = xtb[:, :]
            xks = [("xtb",)]
            st.dma("gpsimd", xsl.rearrange("p (t d) -> p t d", t=4),
                   xi[row0:row0 + 512, :].rearrange("(t p) d -> p t d", p=128),
                   writes=[("xtb",)])
            stk_ = ("stt",)
            st.v("reduce_sum", stt[:, 0:4],
                 xsl.rearrange("p (t d) -> p t d", d=1024), axis=AX.X,
                 reads=xks, writes=[stk_])
            st.s("activation", sqln[:, :], xsl, AF.Square,
                 reads=xks, writes=[("sqln",)])
            st.v("reduce_sum", stt[:, 4:8],
                 sqln[:, :].rearrange("p (t d) -> p t d", d=1024), axis=AX.X,
                 reads=[("sqln",)], writes=[stk_])
            st.v("tensor_scalar", stt[:, 8:12], stt[:, 0:4],
                 1.0 / D, None, ALU.mult, reads=[stk_], writes=[stk_])
            st.v("tensor_scalar", stt[:, 12:16], stt[:, 4:8],
                 1.0 / D, LN_EPS, ALU.mult, ALU.add, reads=[stk_],
                 writes=[stk_])
            st.s("activation", stt[:, 16:20], stt[:, 8:12], AF.Square,
                 reads=[stk_], writes=[stk_])
            st.v("tensor_sub", stt[:, 20:24], stt[:, 12:16], stt[:, 16:20],
                 reads=[stk_], writes=[stk_])
            st.s("activation", stt[:, 24:28], stt[:, 20:24], AF.Ln,
                 reads=[stk_], writes=[stk_])
            st.s("activation", stt[:, 28:32], stt[:, 24:28], AF.Exp,
                 scale=-0.5, reads=[stk_], writes=[stk_])
            st.v("scalar_tensor_tensor", stt[:, 32:36], stt[:, 8:12], -1.0,
                 stt[:, 28:32], ALU.mult, ALU.mult, reads=[stk_],
                 writes=[stk_])
            for i in range(4):
                t = sl * 4 + i
                st.s("activation", xnb[:, i * 1024:(i + 1) * 1024],
                     xsl[:, i * 1024:(i + 1) * 1024], AF.Identity,
                     bias=stt[:, 32 + i:33 + i], scale=stt[:, 28 + i:29 + i],
                     reads=xks + [stk_], writes=[("xnb",)])
            for i in range(4):
                t = sl * 4 + i
                emit_tp8(st, xnb[:, i * 1024:(i + 1) * 1024], ("xnb",),
                         xnf[:, rg * 8192:(rg + 1) * 8192]
                         .rearrange("p (c t) -> p c t", c=8)
                         [:, :, t * 128:(t + 1) * 128],
                         [("xnf", rg)], pt, identb)

        def lngate(g):
            rg = g % 2
            slab_ln(g, 0)
            slab_ln(g, 1)

            def lhs(t, c):
                return xnf[:, rg * 8192 + c * 1024 + t * 128:
                           rg * 8192 + c * 1024 + (t + 1) * 128]

            for t in range(GT):
                for c in range(DC):
                    st.mm(psm[:, t * 8:(t + 1) * 8], lhs(t, c),
                          gws[:, c * 8:(c + 1) * 8], c == 0, c == DC - 1,
                          reads=[("xnf", rg), "gws"], writes=pk("psm", 0, 64))
            st.v("tensor_copy", lgt[:, :], psm[:, 0:64],
                 reads=pk("psm", 0, 64), writes=[("lgt",)])
            gk = ("gsc",)
            v3 = lgt[:, :].rearrange("p (t e) -> p t e", e=8)
            st.v("reduce_max", gsc[:, 0:8], v3, axis=AX.X, negate=True,
                 reads=[("lgt",)], writes=[gk])
            st.v("tensor_tensor", gsc[:, 64:128]
                 .rearrange("p (t e) -> p t e", e=8), v3,
                 bcast(gsc[:, 0:8], 8, 8), ALU.add,
                 reads=[("lgt",), gk], writes=[gk])
            st.s("activation", gsc[:, 64:128], gsc[:, 64:128], AF.Exp,
                 reads=[gk], writes=[gk])
            st.v("reduce_sum", gsc[:, 8:16], gsc[:, 64:128]
                 .rearrange("p (t e) -> p t e", e=8), axis=AX.X,
                 reads=[gk], writes=[gk])
            st.v("reciprocal", gsc[:, 16:24], gsc[:, 8:16],
                 reads=[gk], writes=[gk])
            st.v("tensor_tensor", gsc[:, 128:192]
                 .rearrange("p (t e) -> p t e", e=8),
                 gsc[:, 64:128].rearrange("p (t e) -> p t e", e=8),
                 bcast(gsc[:, 16:24], 8, 8), ALU.mult,
                 reads=[gk], writes=[gk])
            st.v("tensor_tensor", gsc[:, 192:256]
                 .rearrange("p (t e) -> p t e", e=8),
                 gsc[:, 128:192].rearrange("p (t e) -> p t e", e=8),
                 bcast_cols(gbx[:, :], 8), ALU.add,
                 reads=[gk, "gbx"], writes=[gk])
            st.v("reduce_max", gsc[:, 24:32], gsc[:, 192:256]
                 .rearrange("p (t e) -> p t e", e=8), axis=AX.X,
                 reads=[gk], writes=[gk])
            st.v("tensor_tensor", gsc[:, 256:320]
                 .rearrange("p (t e) -> p t e", e=8),
                 gsc[:, 192:256].rearrange("p (t e) -> p t e", e=8),
                 bcast(gsc[:, 24:32], 8, 8), ALU.is_equal,
                 reads=[gk], writes=[gk])
            st.v("scalar_tensor_tensor", gsc[:, 256:320], gsc[:, 256:320],
                 -1e30, gsc[:, 192:256], ALU.mult, ALU.add,
                 reads=[gk], writes=[gk])
            st.v("reduce_max", gsc[:, 32:40], gsc[:, 256:320]
                 .rearrange("p (t e) -> p t e", e=8), axis=AX.X,
                 reads=[gk], writes=[gk])
            st.v("tensor_tensor", gsc[:, 256:320]
                 .rearrange("p (t e) -> p t e", e=8),
                 gsc[:, 192:256].rearrange("p (t e) -> p t e", e=8),
                 bcast(gsc[:, 32:40], 8, 8), ALU.is_ge,
                 reads=[gk], writes=[gk])
            st.v("tensor_mul", cmb[:, rg * 64:(rg + 1) * 64],
                 gsc[:, 128:192], gsc[:, 256:320],
                 reads=[gk], writes=[("cmb", rg)])

        lngate(0)
        for g in range(NG):
            row0 = g * 1024
            rg = g % 2
            st.dma("sync",
                   x1s[:, :].rearrange("p (t d) -> p t d", t=8),
                   xi[row0:row0 + 1024, :].rearrange("(t p) d -> p t d", p=128),
                   writes=[("x1s", t) for t in range(GT)])
            for e in experts:
                if e == NE and g + 1 < NG:
                    b.record()
                def load_quarter(qg):
                    rot = qg % 2
                    st.dma("gpsimd",
                           w1q[:, rot * 4096:(rot + 1) * 4096]
                           .rearrange("p (c i) -> p c i", c=8),
                           wsrc(e, "w1")[:, qg * 512:(qg + 1) * 512]
                           .rearrange("(c p) i -> p c i", p=128),
                           writes=[("w1q", rot)])
                    st.dma("gpsimd",
                           w3q[:, rot * 4096:(rot + 1) * 4096]
                           .rearrange("p (c i) -> p c i", c=8),
                           wsrc(e, "w3")[:, qg * 512:(qg + 1) * 512]
                           .rearrange("(c p) i -> p c i", p=128),
                           writes=[("w3q", rot)])
                load_quarter(0)
                load_quarter(1)
                st.dma("gpsimd",
                       w2b[:, :].rearrange("p (c o) -> p c o", c=16),
                       wsrc(e, "w2").rearrange("(c p) o -> p c o", p=128),
                       writes=["w2b"])
                for qg in range(4):
                    rot = qg % 2
                    if qg >= 2:
                        load_quarter(qg)
                    for sg in range(2):
                        for i4 in range(4):
                            ic = qg * 4 + i4
                            pr = ic % 2
                            ga = pa[:, pr * 512:(pr + 1) * 512]
                            gb_ = pb[:, pr * 512:(pr + 1) * 512]
                            gak = pk("pa", pr * 512, (pr + 1) * 512)
                            gbk = pk("pb", pr * 512, (pr + 1) * 512)
                            io = i4 * 128
                            for c in range(DC):
                                st.mm(ga,
                                      w1q[:, rot * 4096 + c * 512 + io:
                                          rot * 4096 + c * 512 + io + 128],
                                      xnf[:, rg * 8192 + c * 1024 + sg * 512:
                                          rg * 8192 + c * 1024 +
                                          (sg + 1) * 512],
                                      c == 0, c == DC - 1,
                                      reads=[("w1q", rot), ("xnf", rg)],
                                      writes=gak)
                            for c in range(DC):
                                st.mm(gb_,
                                      w3q[:, rot * 4096 + c * 512 + io:
                                          rot * 4096 + c * 512 + io + 128],
                                      xnf[:, rg * 8192 + c * 1024 + sg * 512:
                                          rg * 8192 + c * 1024 +
                                          (sg + 1) * 512],
                                      c == 0, c == DC - 1,
                                      reads=[("w3q", rot), ("xnf", rg)],
                                      writes=gbk)
                            st.s("activation",
                                 hsl[:, pr * 512:(pr + 1) * 512],
                                 ga, AF.Silu, reads=gak, writes=[("hsl", pr)])
                            st.v("tensor_tensor",
                                 hb[:, (sg * 16 + ic) * 512:
                                    (sg * 16 + ic + 1) * 512],
                                 hsl[:, pr * 512:(pr + 1) * 512], gb_,
                                 ALU.mult,
                                 reads=[("hsl", pr)] + gbk,
                                 writes=[("hb", sg, ic)])
                for t in range(GT):
                    sgt = t // 4
                    toff = (t % 4) * 128
                    for oh in range(2):
                        yrot = (t * 2 + oh) % 4
                        py = (pa if yrot < 2 else pb)
                        yo = (yrot % 2) * 512
                        ya = py[:, yo:yo + 512]
                        yk = pk("pa" if yrot < 2 else "pb", yo, yo + 512)
                        for ic in range(16):
                            sgi = sgt * 16 + ic
                            st.mm(ya,
                                  hb[:, sgi * 512 + toff:sgi * 512 + toff + 128],
                                  w2b[:, ic * 1024 + oh * 512:
                                      ic * 1024 + (oh + 1) * 512],
                                  ic == 0, ic == 15,
                                  reads=[("hb", sgt, ic), "w2b"], writes=yk)
                        xsl = x1s[:, t * 1024 + oh * 512:
                                  t * 1024 + (oh + 1) * 512]
                        if e < NE:
                            st.v("scalar_tensor_tensor", xsl, ya,
                                 cmb[:, rg * 64 + t * 8 + e:
                                     rg * 64 + t * 8 + e + 1], xsl,
                                 ALU.mult, ALU.add,
                                 reads=yk + [("cmb", rg), ("x1s", t)],
                                 writes=[("x1s", t)])
                        else:
                            st.v("tensor_add", xsl, xsl, ya,
                                 reads=yk + [("x1s", t)],
                                 writes=[("x1s", t)])
                if e == NE and g + 1 < NG:
                    bufe = b.take()
                    b.record()
                    lngate(g + 1)
                    bufl = b.take()
                    b.interleave(bufe, bufl)
            st.dma("scalar",
                   xo[row0:row0 + 1024, :].rearrange("(t p) d -> p t d", p=128),
                   x1s[:, :].rearrange("p (t d) -> p t d", t=8),
                   reads=[("x1s", t) for t in range(GT)])


def emit_stage_c(st, xi, xo, wq, wk, wv, wo, bq, bk, bv, bo, m_in, consts):
    nc, b = st.nc, st.b
    identb, idf, triu, ones1f, zbf, obf = consts
    SLAB = 4
    SPS = TPS // SLAB
    NSL = NT // SLAB
    with ExitStack() as stk:
        st.stk = stk
        wqa = st.sb("wqa", [128, 8192], BF16)
        wka = st.sb("wka", [128, 8192], BF16)
        wva = st.sb("wva", [128, 8192], BF16)
        woa = st.sb("woa", [128, 8192], BF16)
        bsx = st.sb("bsx", [1, 4096], BF16)
        ms = st.sb("ms", [1, 64], F32)
        scb = st.sb("scb", [128, 16], F32)
        xt = st.sb("xt", [128, 4096], BF16)
        xrs = st.sb("xrs", [128, 1024], F32)
        stt = st.sb("stt", [128, 40], F32)
        sqln = st.sb("sqln", [128, 4096], BF16)
        sq = st.sb("sq", [128, 4096], BF16)
        sqk = st.sb("sqk", [128, 4096], BF16)
        xnb = st.sb("xnb", [128, 4096], BF16)
        xnf = st.sb("xnf", [128, 4096], BF16)
        qf = st.sb("qf", [128, 4096], BF16)
        kf = st.sb("kf", [128, 4096], BF16)
        ssq = st.sb("ssq", [128, 128], F32)
        qnb = st.sb("qnb", [128, 4096], BF16)
        knb = st.sb("knb", [128, 4096], BF16)
        vbb = st.sb("vbb", [128, 4096], BF16)
        qfa = st.sb("qfa", [128, 16384], BF16)
        ssc = st.sb("ssc", [128, 512], BF16)
        ssf = st.sb("ssf", [128, 512], F32)
        afm = st.sb("afm", [128, 1024], BF16)
        pa = st.ps("pa", [128, 1024])
        pb = st.ps("pb", [128, 1024])
        psm = st.ps("psm", [128, 512])
        s0 = st.ps("s0", [128, 512])
        pt = st.ps("pt", [128, 1024], BF16)
        pt2 = st.ps("pt2", [128, 1024], BF16)
        skey = ("Ps0c", 0)

        st.dma("gpsimd", wqa[:, :].rearrange("p (c n) -> p c n", c=8),
               wq.rearrange("(c p) n -> p c n", p=128), writes=["wqa"])
        st.dma("gpsimd", wka[:, :].rearrange("p (c n) -> p c n", c=8),
               wk.rearrange("(c p) n -> p c n", p=128), writes=["wka"])
        st.dma("gpsimd", wva[:, :].rearrange("p (c n) -> p c n", c=8),
               wv.rearrange("(c p) n -> p c n", p=128), writes=["wva"])
        st.dma("gpsimd", woa[:, :].rearrange("p (c n) -> p c n", c=8),
               wo.rearrange("(c p) n -> p c n", p=128), writes=["woa"])
        st.dma("gpsimd", bsx[0:1, 0:1024], bq[:, :], writes=["bsx"])
        st.dma("gpsimd", bsx[0:1, 1024:2048], bk[:, :], writes=["bsx"])
        st.dma("gpsimd", bsx[0:1, 2048:3072], bv[:, :], writes=["bsx"])
        st.dma("gpsimd", bsx[0:1, 3072:4096], bo[:, :], writes=["bsx"])
        st.dma("sync", ms[0:1, 0:16], m_in[:, :], writes=["ms"])
        st.s("activation", ms[0:1, 16:32], ms[0:1, 0:16], AF.Exp, scale=-1.0,
             reads=["ms"], writes=["ms"])
        st.v("tensor_scalar", ms[0:1, 16:32], ms[0:1, 16:32], 1.0, None,
             ALU.add, reads=["ms"], writes=["ms"])
        st.v("reciprocal", ms[0:1, 32:48], ms[0:1, 16:32],
             reads=["ms"], writes=["ms"])
        st.s("activation", ms[0:1, 48:64], ms[0:1, 32:48], AF.Exp,
             scale=-float(math.log(L)), reads=["ms"], writes=["ms"])
        st.mm(psm[:, 0:16], ones1f[0:1, :], ms[0:1, 48:64], True, True,
              reads=["ones1f", "ms"], writes=pk("psm", 0, 16))
        st.v("tensor_copy", scb[:, :], psm[:, 0:16],
             reads=pk("psm", 0, 16), writes=["scb"])

        def p1a(s):
            r = 0
            bqi = (s * SLAB) // TPS
            t0 = (s * SLAB) % TPS
            row0 = bqi * L + t0 * 128
            xk = ("xt",)
            xsl = xt[:, :]
            st.dma("gpsimd", xsl.rearrange("p (t d) -> p t d", t=4),
                   xi[row0:row0 + 512, :].rearrange("(t p) d -> p t d", p=128),
                   writes=[xk])
            so = 0
            stk_ = ("stt",)
            st.v("reduce_sum", stt[:, so:so + 4],
                 xsl.rearrange("p (t d) -> p t d", d=1024), axis=AX.X,
                 reads=[xk], writes=[stk_])
            st.s("activation", sqln[:, :], xsl, AF.Square,
                 reads=[xk], writes=[("sqln",)])
            st.v("reduce_sum", stt[:, so + 4:so + 8],
                 sqln[:, :].rearrange("p (t d) -> p t d", d=1024), axis=AX.X,
                 reads=[("sqln",)], writes=[stk_])
            st.v("tensor_scalar", stt[:, so + 8:so + 12], stt[:, so:so + 4],
                 1.0 / D, None, ALU.mult, reads=[stk_], writes=[stk_])
            st.v("tensor_scalar", stt[:, so + 12:so + 16],
                 stt[:, so + 4:so + 8], 1.0 / D, LN_EPS, ALU.mult, ALU.add,
                 reads=[stk_], writes=[stk_])
            st.s("activation", stt[:, so + 16:so + 20], stt[:, so + 8:so + 12],
                 AF.Square, reads=[stk_], writes=[stk_])
            st.v("tensor_sub", stt[:, so + 20:so + 24], stt[:, so + 12:so + 16],
                 stt[:, so + 16:so + 20], reads=[stk_], writes=[stk_])
            st.s("activation", stt[:, so + 24:so + 28], stt[:, so + 20:so + 24],
                 AF.Ln, reads=[stk_], writes=[stk_])
            st.s("activation", stt[:, so + 28:so + 32], stt[:, so + 24:so + 28],
                 AF.Exp, scale=-0.5, reads=[stk_], writes=[stk_])
            st.v("scalar_tensor_tensor", stt[:, so + 32:so + 36],
                 stt[:, so + 8:so + 12], -1.0, stt[:, so + 28:so + 32],
                 ALU.mult, ALU.mult, reads=[stk_], writes=[stk_])
            for t in range(SLAB):
                st.s("activation", xnb[:, t * 1024:(t + 1) * 1024],
                     xsl[:, t * 1024:(t + 1) * 1024], AF.Identity,
                     bias=stt[:, so + 32 + t:so + 33 + t],
                     scale=stt[:, so + 28 + t:so + 29 + t],
                     reads=[xk, stk_], writes=[("xnb",)])
            for t in range(SLAB):
                emit_tp8(st, xnb[:, t * 1024:(t + 1) * 1024], ("xnb",),
                         xnf[:, t * 1024:(t + 1) * 1024]
                         .rearrange("p (c t) -> p c t", c=8),
                         [("xnf",)], pt, identb)

        def p1b(s):
            xfk = ("xnf",)

            def lhs(t, c):
                return xnf[:, t * 1024 + c * 128:t * 1024 + (c + 1) * 128]

            for t in range(SLAB):
                for half in range(2):
                    for c in range(DC):
                        st.mm(pa[:, half * 512:(half + 1) * 512], lhs(t, c),
                              wqa[:, c * 1024 + half * 512:
                                  c * 1024 + (half + 1) * 512],
                              c == 0, False,
                              reads=[xfk, "wqa"],
                              writes=pk("pa", half * 512, (half + 1) * 512))
                    st.mm(pa[:, half * 512:(half + 1) * 512], obf[0:1, 0:128],
                          bsx[0:1, half * 512:(half + 1) * 512], False, True,
                          reads=["obf", "bsx"],
                          writes=pk("pa", half * 512, (half + 1) * 512))
                for half in range(2):
                    for c in range(DC):
                        st.mm(pb[:, half * 512:(half + 1) * 512], lhs(t, c),
                              wka[:, c * 1024 + half * 512:
                                  c * 1024 + (half + 1) * 512],
                              c == 0, False,
                              reads=[xfk, "wka"],
                              writes=pk("pb", half * 512, (half + 1) * 512))
                    st.mm(pb[:, half * 512:(half + 1) * 512], obf[0:1, 0:128],
                          bsx[0:1, 1024 + half * 512:1024 + (half + 1) * 512],
                          False, True,
                          reads=["obf", "bsx"],
                          writes=pk("pb", half * 512, (half + 1) * 512))
                st.v("tensor_copy", qf[:, t * 1024:(t + 1) * 1024], pa[:, :],
                     reads=pk("pa", 0, 1024), writes=[("qf",)])
                st.s("copy", kf[:, t * 1024:(t + 1) * 1024], pb[:, :],
                     reads=pk("pb", 0, 1024), writes=[("kf",)])
                for half in range(2):
                    for c in range(DC):
                        st.mm(pa[:, half * 512:(half + 1) * 512], lhs(t, c),
                              wva[:, c * 1024 + half * 512:
                                  c * 1024 + (half + 1) * 512],
                              c == 0, False,
                              reads=[xfk, "wva"],
                              writes=pk("pa", half * 512, (half + 1) * 512))
                    st.mm(pa[:, half * 512:(half + 1) * 512], obf[0:1, 0:128],
                          bsx[0:1, 2048 + half * 512:2048 + (half + 1) * 512],
                          False, True,
                          reads=["obf", "bsx"],
                          writes=pk("pa", half * 512, (half + 1) * 512))
                st.v("tensor_copy", vbb[:, t * 1024:(t + 1) * 1024], pa[:, :],
                     reads=pk("pa", 0, 1024), writes=[("vbb",)])

        def p1c(s):
            bqi = (s * SLAB) // TPS
            t0 = (s * SLAB) % TPS
            kq_, kk_ = ("ssq", "q"), ("ssq", "k")
            b.record()
            st.s("activation", sq[:, :], qf[:, :], AF.Square,
                 reads=[("qf",)], writes=[("sq",)])
            st.v("reduce_sum", ssq[:, 0:64],
                 sq[:, :].rearrange("p (h e) -> p h e", e=64), axis=AX.X,
                 reads=[("sq",)], writes=[kq_])
            st.s("activation", ssq[:, 0:64], ssq[:, 0:64], AF.Ln,
                 reads=[kq_], writes=[kq_])
            st.s("activation", ssq[:, 0:64], ssq[:, 0:64], AF.Exp, scale=-0.5,
                 reads=[kq_], writes=[kq_])
            st.b.op("gpsimd", lambda e: e.tensor_tensor(
                qnb[:, :].rearrange("p (h e) -> p h e", e=64),
                qf[:, :].rearrange("p (h e) -> p h e", e=64),
                bcast(ssq[:, 0:64], 64, 64), ALU.mult),
                reads=[("qf",), kq_], writes=[("qnb",)])
            cq = b.take()
            b.record()
            st.s("activation", sqk[:, :], kf[:, :], AF.Square,
                 reads=[("kf",)], writes=[("sqk",)])
            st.v("reduce_sum", ssq[:, 64:128],
                 sqk[:, :].rearrange("p (h e) -> p h e", e=64), axis=AX.X,
                 reads=[("sqk",)], writes=[kk_])
            st.s("activation", ssq[:, 64:128], ssq[:, 64:128], AF.Ln,
                 reads=[kk_], writes=[kk_])
            st.s("activation", ssq[:, 64:128], ssq[:, 64:128], AF.Exp,
                 scale=-0.5, reads=[kk_], writes=[kk_])
            st.b.op("gpsimd", lambda e: e.tensor_tensor(
                knb[:, :].rearrange("p (h e) -> p h e", e=64),
                kf[:, :].rearrange("p (h e) -> p h e", e=64),
                bcast(ssq[:, 64:128], 64, 64), ALU.mult),
                reads=[("kf",), kk_], writes=[("knb",)])
            ck = b.take()
            b.interleave(cq, ck)
            for t in range(SLAB):
                emit_tp8(st, qnb[:, t * 1024:(t + 1) * 1024], ("qnb",),
                         qfa[:, :].rearrange("p (c t) -> p c t", c=8)
                         [:, :, (t0 + t) * 128:(t0 + t + 1) * 128],
                         [("qfa", s % SPS)], pt2, identb)
            if t0 == 0:
                st.b.op("gpsimd", lambda e: e.memset(ssf[:, :], 0.0),
                        reads=[("ssf",)], writes=[("ssf",)])
            for h in range(H):
                par = h % 2
                hc = h // 2
                rl = par * 64
                for t in range(SLAB):
                    st.mm(s0[rl:rl + 64, hc * 64:(hc + 1) * 64],
                          knb[:, t * 1024 + h * 64:t * 1024 + (h + 1) * 64],
                          vbb[:, t * 1024 + h * 64:t * 1024 + (h + 1) * 64],
                          t == 0, t == SLAB - 1,
                          reads=[("knb",), ("vbb",)], writes=[skey])
            st.v("tensor_add", ssf[:, :], ssf[:, :], s0[:, :],
                 reads=[skey, ("ssf",)], writes=[("ssf",)])
            if (t0 + SLAB) == TPS:
                for h in range(H):
                    par = h % 2
                    hc = h // 2
                    rl = par * 64
                    st.v("tensor_scalar",
                         ssc[rl:rl + 64, hc * 64:(hc + 1) * 64],
                         ssf[rl:rl + 64, hc * 64:(hc + 1) * 64],
                         scb[rl:rl + 64, h:h + 1], None, ALU.mult,
                         reads=[("ssf",), "scb"], writes=[("ssc", bqi)])

        def p2(g):
            bqi, tt = divmod(g, TPS)
            row0 = bqi * L + tt * 128
            for h in range(H):
                par = h % 2
                hc = h // 2
                rl = par * 64
                st.mm(pb[rl:rl + 64, hc * 128:(hc + 1) * 128],
                      ssc[rl:rl + 64, hc * 64:(hc + 1) * 64],
                      qfa[rl:rl + 64,
                          hc * 2048 + tt * 128:hc * 2048 + (tt + 1) * 128],
                      True, True,
                      reads=[("ssc", bqi), ("qfa", "all")],
                      writes=pk("pb", hc * 128, (hc + 1) * 128))
            st.v("tensor_copy", afm[:, :], pb[:, :],
                 reads=pk("pb", 0, 1024), writes=[("afm",)])
            for half in range(2):
                for c in range(DC):
                    st.mm(pa[:, half * 512:(half + 1) * 512],
                          afm[:, c * 128:(c + 1) * 128],
                          woa[:, c * 1024 + half * 512:
                              c * 1024 + (half + 1) * 512],
                          c == 0, False,
                          reads=[("afm",), "woa"],
                          writes=pk("pa", half * 512, (half + 1) * 512))
                st.mm(pa[:, half * 512:(half + 1) * 512], obf[0:1, 0:128],
                      bsx[0:1, 3072 + half * 512:3072 + (half + 1) * 512],
                      False, True,
                      reads=["obf", "bsx"],
                      writes=pk("pa", half * 512, (half + 1) * 512))
            st.dma("sync", xrs[:, :], xi[row0:row0 + 128, :],
                   writes=[("xrs",)])
            st.v("tensor_add", xrs[:, :], xrs[:, :], pa[:, :],
                 reads=[("xrs",)] + pk("pa", 0, 1024), writes=[("xrs",)])
            st.dma("scalar", xo[row0:row0 + 128, :], xrs[:, :],
                   reads=[("xrs",)])

        for bqi in range(SEQ):
            s0_ = bqi * SPS
            p1a(s0_)
            for s in range(s0_, s0_ + SPS):
                p1b(s)
                b.record()
                p1c(s)
                buf1 = b.take()
                b.record()
                if s + 1 < s0_ + SPS:
                    p1a(s + 1)
                buf2 = b.take()
                b.interleave(buf1, buf2)
            for tt in range(TPS):
                p2(bqi * TPS + tt)


def emit_stage_d(st, xi, xo, wlt, blin, gwt, la, ra, consts):
    nc, b = st.nc, st.b
    identb, idf, triu, ones1f, zbf, obf = consts
    SLAB = 4
    NSL = NT // SLAB
    with ExitStack() as stk:
        st.stk = stk
        wls = st.sb("wls", [128, 8192], BF16)
        bls = st.sb("bls", [1, 1024], BF16)
        gws = st.sb("gws", [128, 64], BF16)
        las = st.sb("las", [128, 256], BF16)
        ras = st.sb("ras", [32, 1024], BF16)
        xt = st.sb("xt", [128, 4096], F32)
        xrs = st.sb("xrs", [128, 2048], F32)
        stt = st.sb("stt", [128, 40], F32)
        sqln = st.sb("sqln", [128, 4096], BF16)
        xnb = st.sb("xnb", [128, 4096], BF16)
        xnf = st.sb("xnf", [128, 2 * 4096], BF16)
        lgt = st.sb("lgt", [128, 32], F32)
        gsc = st.sb("gsc", [128, 160], F32)
        cw = st.sb("cw", [128, 32], F32)
        tsb = st.sb("tsb", [128, 128], BF16)
        tfm = st.sb("tfm", [32, 512], BF16)
        pa = st.ps("pa", [128, 1024])
        pb = st.ps("pb", [128, 1024])
        psm = st.ps("psm", [128, 512])
        pt = st.ps("pt", [128, 1024], BF16)
        pt2 = st.ps("pt2", [128, 1024], BF16)

        st.dma("gpsimd", wls[:, :].rearrange("p (c n) -> p c n", c=8),
               wlt.rearrange("(c p) n -> p c n", p=128), writes=["wls"])
        st.dma("gpsimd", bls[:, :], blin[:, :], writes=["bls"])
        st.dma("gpsimd", gws[:, :].rearrange("p (c n) -> p c n", c=8),
               gwt.rearrange("(c p) n -> p c n", p=128), writes=["gws"])
        st.dma("gpsimd", las[:, :].rearrange("p (c n) -> p c n", c=8),
               la.rearrange("(c p) n -> p c n", p=128), writes=["las"])
        st.dma("gpsimd", ras[:, :], ra[:, :], writes=["ras"])

        def p1a(s):
            t00 = s * SLAB
            row0 = t00 * 128
            xk = ("xt",)
            xsl = xt[:, :]
            st.dma("sync", xsl.rearrange("p (t d) -> p t d", t=4),
                   xi[row0:row0 + 512, :].rearrange("(t p) d -> p t d", p=128),
                   writes=[xk])
            stk_ = ("stt",)
            st.v("reduce_sum", stt[:, 0:4],
                 xsl.rearrange("p (t d) -> p t d", d=1024), axis=AX.X,
                 reads=[xk], writes=[stk_])
            st.s("activation", sqln[:, :], xsl, AF.Square,
                 reads=[xk], writes=[("sqln",)])
            st.v("reduce_sum", stt[:, 4:8],
                 sqln[:, :].rearrange("p (t d) -> p t d", d=1024), axis=AX.X,
                 reads=[("sqln",)], writes=[stk_])
            st.v("tensor_scalar", stt[:, 8:12], stt[:, 0:4],
                 1.0 / D, None, ALU.mult, reads=[stk_], writes=[stk_])
            st.v("tensor_scalar", stt[:, 12:16], stt[:, 4:8],
                 1.0 / D, LN_EPS, ALU.mult, ALU.add, reads=[stk_], writes=[stk_])
            st.s("activation", stt[:, 16:20], stt[:, 8:12],
                 AF.Square, reads=[stk_], writes=[stk_])
            st.v("tensor_sub", stt[:, 20:24], stt[:, 12:16], stt[:, 16:20],
                 reads=[stk_], writes=[stk_])
            st.s("activation", stt[:, 24:28], stt[:, 20:24], AF.Ln,
                 reads=[stk_], writes=[stk_])
            st.s("activation", stt[:, 28:32], stt[:, 24:28], AF.Exp,
                 scale=-0.5, reads=[stk_], writes=[stk_])
            st.v("scalar_tensor_tensor", stt[:, 32:36], stt[:, 8:12], -1.0,
                 stt[:, 28:32], ALU.mult, ALU.mult, reads=[stk_], writes=[stk_])
            for t in range(SLAB):
                st.s("activation", xnb[:, t * 1024:(t + 1) * 1024],
                     xsl[:, t * 1024:(t + 1) * 1024], AF.Identity,
                     bias=stt[:, 32 + t:33 + t], scale=stt[:, 28 + t:29 + t],
                     reads=[xk, stk_], writes=[("xnb",)])
            for t in range(SLAB):
                emit_tp8(st, xnb[:, t * 1024:(t + 1) * 1024], ("xnb",),
                         xnf[:, (s % 2) * 4096 + t * 1024:
                             (s % 2) * 4096 + (t + 1) * 1024]
                         .rearrange("p (c t) -> p c t", c=8),
                         [("xnf", s % 2)], pt, identb)

        def p1b(s):
            rs = s % 2
            xfk = ("xnf", rs)

            def lhs(t, c):
                return xnf[:, rs * 4096 + t * 1024 + c * 128:
                           rs * 4096 + t * 1024 + (c + 1) * 128]

            for t in range(SLAB):
                for c in range(DC):
                    st.mm(psm[:, t * 128:t * 128 + 32], lhs(t, c),
                          las[:, c * 32:(c + 1) * 32], c == 0, c == DC - 1,
                          reads=[xfk, "las"], writes=pk("psm", 0, 512))
                for c in range(DC):
                    st.mm(psm[:, t * 128 + 32:t * 128 + 40], lhs(t, c),
                          gws[:, c * 8:(c + 1) * 8], c == 0, c == DC - 1,
                          reads=[xfk, "gws"], writes=pk("psm", 0, 512))
            st.v("tensor_copy",
                 lgt[:, :].rearrange("p (t e) -> p t e", e=8),
                 psm[:, :].rearrange("p (t x) -> p t x", x=128)[:, :, 32:40],
                 reads=pk("psm", 0, 512), writes=[("lgt",)])
            gk = ("gsc",)
            v3 = lgt[:, :].rearrange("p (t e) -> p t e", e=8)
            st.v("reduce_max", gsc[:, 0:4], v3, axis=AX.X, negate=True,
                 reads=[("lgt",)], writes=[gk])
            st.v("tensor_tensor", gsc[:, 32:64]
                 .rearrange("p (t e) -> p t e", e=8), v3,
                 bcast(gsc[:, 0:4], 4, 8), ALU.add,
                 reads=[("lgt",), gk], writes=[gk])
            st.s("activation", gsc[:, 32:64], gsc[:, 32:64], AF.Exp,
                 reads=[gk], writes=[gk])
            st.v("reduce_sum", gsc[:, 4:8], gsc[:, 32:64]
                 .rearrange("p (t e) -> p t e", e=8), axis=AX.X,
                 reads=[gk], writes=[gk])
            st.v("reciprocal", gsc[:, 8:12], gsc[:, 4:8],
                 reads=[gk], writes=[gk])
            st.v("tensor_tensor", gsc[:, 64:96]
                 .rearrange("p (t e) -> p t e", e=8),
                 gsc[:, 32:64].rearrange("p (t e) -> p t e", e=8),
                 bcast(gsc[:, 8:12], 4, 8), ALU.mult,
                 reads=[gk], writes=[gk])
            st.v("reduce_max", gsc[:, 12:16], gsc[:, 64:96]
                 .rearrange("p (t e) -> p t e", e=8), axis=AX.X,
                 reads=[gk], writes=[gk])
            st.v("tensor_tensor", gsc[:, 96:128]
                 .rearrange("p (t e) -> p t e", e=8),
                 gsc[:, 64:96].rearrange("p (t e) -> p t e", e=8),
                 bcast(gsc[:, 12:16], 4, 8), ALU.is_equal,
                 reads=[gk], writes=[gk])
            st.v("scalar_tensor_tensor", gsc[:, 128:160], gsc[:, 96:128],
                 -1e30, gsc[:, 64:96], ALU.mult, ALU.add,
                 reads=[gk], writes=[gk])
            st.v("reduce_max", gsc[:, 16:20], gsc[:, 128:160]
                 .rearrange("p (t e) -> p t e", e=8), axis=AX.X,
                 reads=[gk], writes=[gk])
            st.v("tensor_tensor", gsc[:, 96:128]
                 .rearrange("p (t e) -> p t e", e=8),
                 gsc[:, 64:96].rearrange("p (t e) -> p t e", e=8),
                 bcast(gsc[:, 16:20], 4, 8), ALU.is_ge,
                 reads=[gk], writes=[gk])
            st.v("tensor_mul", cw[:, :], gsc[:, 64:96], gsc[:, 96:128],
                 reads=[gk], writes=[("cw",)])
            for t in range(SLAB):
                st.v("tensor_tensor",
                     tsb[:, t * 32:(t + 1) * 32]
                     .rearrange("p (e r) -> p e r", r=4),
                     psm[:, t * 128:t * 128 + 32]
                     .rearrange("p (e r) -> p e r", r=4),
                     bcast(cw[:, t * 8:(t + 1) * 8], 8, 4), ALU.mult,
                     reads=pk("psm", 0, 512) + [("cw",)], writes=[("tsb",)])
            for t in range(SLAB):
                st.tr(pt2[0:32, t * 128:(t + 1) * 128],
                      tsb[:, t * 32:(t + 1) * 32], identb[:, :],
                      reads=[("tsb",), "identb"], writes=[("PPT2", 0)])
                st.v("tensor_copy", tfm[0:32, t * 128:(t + 1) * 128],
                     pt2[0:32, t * 128:(t + 1) * 128],
                     reads=[("PPT2", 0)], writes=[("tfm",)])
            for t in range(SLAB):
                pbase = pa if t % 2 == 0 else pb
                pnm = "pa" if t % 2 == 0 else "pb"
                for half in range(2):
                    for c in range(DC):
                        st.mm(pbase[:, half * 512:(half + 1) * 512], lhs(t, c),
                              wls[:, c * 1024 + half * 512:
                                  c * 1024 + (half + 1) * 512],
                              c == 0, False,
                              reads=[xfk, "wls"],
                              writes=pk(pnm, half * 512, (half + 1) * 512))
                    st.mm(pbase[:, half * 512:(half + 1) * 512],
                          obf[0:1, 0:128],
                          bls[0:1, half * 512:(half + 1) * 512], False, False,
                          reads=["obf", "bls"],
                          writes=pk(pnm, half * 512, (half + 1) * 512))
                    st.mm(pbase[:, half * 512:(half + 1) * 512],
                          tfm[0:32, t * 128:(t + 1) * 128],
                          ras[0:32, half * 512:(half + 1) * 512], False, True,
                          reads=[("tfm",), "ras"],
                          writes=pk(pnm, half * 512, (half + 1) * 512))
                g = s * SLAB + t
                row0 = g * 128
                ra_ = t % 2
                xra = xrs[:, ra_ * 1024:(ra_ + 1) * 1024]
                st.dma("sync", xra, xi[row0:row0 + 128, :],
                       writes=[("xrs", ra_)])
                st.v("tensor_add", xra, xra, pbase[:, :],
                     reads=[("xrs", ra_)] + pk(pnm, 0, 1024),
                     writes=[("xrs", ra_)])
                st.dma("scalar", xo[row0:row0 + 128, :], xra,
                       reads=[("xrs", ra_)])

        p1a(0)
        for s in range(NSL):
            b.record()
            p1b(s)
            buf1 = b.take()
            b.record()
            if s + 1 < NSL:
                p1a(s + 1)
            buf2 = b.take()
            b.interleave(buf1, buf2)


def _mk_consts(nc):
    identb = nc.dram_tensor("identb", [128, 128], BF16,
                            kind="ExternalInput").ap()
    idf = nc.dram_tensor("idf", [128, 128], F32, kind="ExternalInput").ap()
    triu = nc.dram_tensor("triu", [128, 128], F32, kind="ExternalInput").ap()
    ones1f = nc.dram_tensor("ones1f", [1, 128], F32,
                            kind="ExternalInput").ap()
    zbf = nc.dram_tensor("zbf", [1, 128], BF16, kind="ExternalInput").ap()
    obf = nc.dram_tensor("obf", [1, 512], BF16, kind="ExternalInput").ap()
    return identb, idf, triu, ones1f, zbf, obf


def _load_consts(st, consts):
    identb, idf, triu, ones1f, zbf, obf = consts
    sb_identb = st.sb("c_identb", [128, 128], BF16)
    sb_idf = st.sb("c_idf", [128, 128], F32)
    sb_triu = st.sb("c_triu", [128, 128], F32)
    sb_ones1f = st.sb("c_ones1f", [1, 128], F32)
    sb_zbf = st.sb("c_zbf", [1, 128], BF16)
    sb_obf = st.sb("c_obf", [1, 512], BF16)
    st.dma("sync", sb_identb[:, :], identb, writes=["identb"])
    st.dma("sync", sb_idf[:, :], idf, writes=["idf"])
    st.dma("sync", sb_triu[:, :], triu, writes=["triu"])
    st.dma("sync", sb_ones1f[:, :], ones1f, writes=["ones1f"])
    st.dma("sync", sb_zbf[:, :], zbf, writes=["zbf"])
    st.dma("sync", sb_obf[:, :], obf, writes=["obf"])
    return (sb_identb, sb_idf, sb_triu, sb_ones1f, sb_zbf, sb_obf)


def _const_arrays():
    return {
        "identb": np.eye(128, dtype=BFNP),
        "idf": np.eye(128, dtype=np.float32),
        "triu": np.triu(np.ones((128, 128), dtype=np.float32)),
        "ones1f": np.ones((1, 128), dtype=np.float32),
        "zbf": np.zeros((1, 128), dtype=BFNP),
        "obf": np.ones((1, 512), dtype=BFNP),
    }


def _din(nc, name, shape):
    return nc.dram_tensor(name, list(shape), F32, kind="ExternalInput").ap()


def _build_stage_program(stage, shapes):
    nc = bass.Bass("TRN2", target_bir_lowering=False)
    consts_d = _mk_consts(nc)
    xi = _din(nc, "xi", [T, D])
    xo = nc.dram_tensor("xo", [T, D], F32, kind="ExternalOutput").ap()
    bld = Builder()
    st = St(nc, bld)
    with ExitStack() as cstk:
        st.stk = cstk
        consts = _load_consts(st, consts_d)
        if stage == "A":
            wq = _din(nc, "a_wq", [D, D])
            wk = _din(nc, "a_wk", [D, D])
            wv = _din(nc, "a_wv", [D, D])
            wo = _din(nc, "a_wo", [D, D])
            wb = _din(nc, "a_wb", [D, H])
            bb = _din(nc, "a_bb", [1, H])
            emit_stage_a(st, xi, xo, wq, wk, wv, wo, wb, bb, consts)
        elif stage == "B":
            w1t = _din(nc, "b_w1t", [NE, D, INTER])
            w2t = _din(nc, "b_w2t", [NE, INTER, D])
            w3t = _din(nc, "b_w3t", [NE, D, INTER])
            sw1t = _din(nc, "b_sw1t", [D, INTER])
            sw2t = _din(nc, "b_sw2t", [INTER, D])
            sw3t = _din(nc, "b_sw3t", [D, INTER])
            gwt = _din(nc, "b_gwt", [D, NE])
            gb = _din(nc, "b_gb", [1, NE])
            emit_stage_b(st, xi, xo, w1t, w2t, w3t, sw1t, sw2t, sw3t,
                         gwt, gb, consts)
        elif stage == "C":
            wq = _din(nc, "c_wq", [D, D])
            wk = _din(nc, "c_wk", [D, D])
            wv = _din(nc, "c_wv", [D, D])
            wo = _din(nc, "c_wo", [D, D])
            bq = _din(nc, "c_bq", [1, D])
            bk = _din(nc, "c_bk", [1, D])
            bv = _din(nc, "c_bv", [1, D])
            bo = _din(nc, "c_bo", [1, D])
            m_in = _din(nc, "c_m", [1, H])
            emit_stage_c(st, xi, xo, wq, wk, wv, wo, bq, bk, bv, bo,
                         m_in, consts)
        elif stage == "D":
            wlt = _din(nc, "d_wlt", [D, D])
            blin = _din(nc, "d_blin", [1, D])
            gwt = _din(nc, "d_gwt", [D, NE])
            la = _din(nc, "d_la", [D, 32])
            ra = _din(nc, "d_ra", [32, D])
            emit_stage_d(st, xi, xo, wlt, blin, gwt, la, ra, consts)
        bld.finalize(nc)
    return nc


def _build_fused_program():
    nc = bass.Bass("TRN2", target_bir_lowering=False)
    consts_d = _mk_consts(nc)
    xi = _din(nc, "xi", [T, D])
    xm1 = nc.dram_tensor("xm1", [T, D], F32).ap()
    xm2 = nc.dram_tensor("xm2", [T, D], F32).ap()
    xm3 = nc.dram_tensor("xm3", [T, D], F32).ap()
    xo = nc.dram_tensor("xo", [T, D], F32, kind="ExternalOutput").ap()
    a_wq = _din(nc, "a_wq", [D, D])
    a_wk = _din(nc, "a_wk", [D, D])
    a_wv = _din(nc, "a_wv", [D, D])
    a_wo = _din(nc, "a_wo", [D, D])
    a_wb = _din(nc, "a_wb", [D, H])
    a_bb = _din(nc, "a_bb", [1, H])
    b_w1t = _din(nc, "b_w1t", [NE, D, INTER])
    b_w2t = _din(nc, "b_w2t", [NE, INTER, D])
    b_w3t = _din(nc, "b_w3t", [NE, D, INTER])
    b_sw1t = _din(nc, "b_sw1t", [D, INTER])
    b_sw2t = _din(nc, "b_sw2t", [INTER, D])
    b_sw3t = _din(nc, "b_sw3t", [D, INTER])
    b_gwt = _din(nc, "b_gwt", [D, NE])
    b_gb = _din(nc, "b_gb", [1, NE])
    c_wq = _din(nc, "c_wq", [D, D])
    c_wk = _din(nc, "c_wk", [D, D])
    c_wv = _din(nc, "c_wv", [D, D])
    c_wo = _din(nc, "c_wo", [D, D])
    c_bq = _din(nc, "c_bq", [1, D])
    c_bk = _din(nc, "c_bk", [1, D])
    c_bv = _din(nc, "c_bv", [1, D])
    c_bo = _din(nc, "c_bo", [1, D])
    c_m = _din(nc, "c_m", [1, H])
    d_wlt = _din(nc, "d_wlt", [D, D])
    d_blin = _din(nc, "d_blin", [1, D])
    d_gwt = _din(nc, "d_gwt", [D, NE])
    d_la = _din(nc, "d_la", [D, 32])
    d_ra = _din(nc, "d_ra", [32, D])
    bld = Builder()
    st = St(nc, bld)
    with ExitStack() as cstk:
        st.stk = cstk
        consts = _load_consts(st, consts_d)
        st.stk = None
        st.prefix = "sa_"
        emit_stage_a(st, xi, xm1, a_wq, a_wk, a_wv, a_wo, a_wb, a_bb, consts)
        bld.barrier()
        st.prefix = "sb_"
        emit_stage_b(st, xm1, xm2, b_w1t, b_w2t, b_w3t, b_sw1t, b_sw2t,
                     b_sw3t, b_gwt, b_gb, consts)
        bld.barrier()
        st.prefix = "sc_"
        emit_stage_c(st, xm2, xm3, c_wq, c_wk, c_wv, c_wo, c_bq, c_bk,
                     c_bv, c_bo, c_m, consts)
        bld.barrier()
        st.prefix = "sd_"
        emit_stage_d(st, xm3, xo, d_wlt, d_blin, d_gwt, d_la, d_ra, consts)
        st.stk = cstk
        bld.finalize(nc)
    return nc


_PROG_CACHE = {}


def _get_prog(stage):
    if stage not in _PROG_CACHE:
        if stage == "FUSED":
            _PROG_CACHE[stage] = _build_fused_program()
        else:
            _PROG_CACHE[stage] = _build_stage_program(stage, None)
    return _PROG_CACHE[stage]


def _host_weights(inputs):
    f = np.ascontiguousarray
    w = {}
    w["a_wq"] = f(inputs["L0_attn_Wq"])
    w["a_wk"] = f(inputs["L0_attn_Wk"])
    w["a_wv"] = f(inputs["L0_attn_Wv"])
    w["a_wo"] = f(inputs["L0_attn_Wo"])
    w["a_wb"] = f(inputs["L0_attn_Wb"])
    w["a_bb"] = f(inputs["L0_attn_bb"].reshape(1, H))
    w["b_w1t"] = f(inputs["L0_ffn_w1"].transpose(0, 2, 1))
    w["b_w2t"] = f(inputs["L0_ffn_w2"].transpose(0, 2, 1))
    w["b_w3t"] = f(inputs["L0_ffn_w3"].transpose(0, 2, 1))
    w["b_sw1t"] = f(inputs["L0_ffn_sw1"].T)
    w["b_sw2t"] = f(inputs["L0_ffn_sw2"].T)
    w["b_sw3t"] = f(inputs["L0_ffn_sw3"].T)
    w["b_gwt"] = f(inputs["L0_ffn_gate_w"].T)
    w["b_gb"] = f(inputs["L0_ffn_gate_bias"].reshape(1, NE))
    w["c_wq"] = f(inputs["L1_attn_Wq"])
    w["c_wk"] = f(inputs["L1_attn_Wk"])
    w["c_wv"] = f(inputs["L1_attn_Wv"])
    w["c_wo"] = f(inputs["L1_attn_Wo"])
    w["c_bq"] = f(inputs["L1_attn_bq"].reshape(1, D))
    w["c_bk"] = f(inputs["L1_attn_bk"].reshape(1, D))
    w["c_bv"] = f(inputs["L1_attn_bv"].reshape(1, D))
    w["c_bo"] = f(inputs["L1_attn_bo"].reshape(1, D))
    w["c_m"] = f(inputs["L1_attn_m"].reshape(1, H))
    w["d_wlt"] = f(inputs["L1_ffn_W_lin"].T)
    w["d_blin"] = f(inputs["L1_ffn_b_lin"].reshape(1, D))
    w["d_gwt"] = f(inputs["L1_ffn_gate_w"].T)
    w["d_la"] = f(inputs["L1_ffn_left"].transpose(1, 0, 2).reshape(D, 32))
    w["d_ra"] = f(inputs["L1_ffn_right"].reshape(32, D))
    return w


_STAGE_KEYS = {
    "A": ["a_wq", "a_wk", "a_wv", "a_wo", "a_wb", "a_bb"],
    "B": ["b_w1t", "b_w2t", "b_w3t", "b_sw1t", "b_sw2t", "b_sw3t",
          "b_gwt", "b_gb"],
    "C": ["c_wq", "c_wk", "c_wv", "c_wo", "c_bq", "c_bk", "c_bv", "c_bo",
          "c_m"],
    "D": ["d_wlt", "d_blin", "d_gwt", "d_la", "d_ra"],
}


def run_stages(x_cores, weights, stages=("A", "B", "C", "D")):
    consts = _const_arrays()
    for stage in stages:
        nc = _get_prog(stage)
        in_maps = []
        for ci in range(NCORES):
            m = {"xi": x_cores[ci]}
            m.update(consts)
            for k in _STAGE_KEYS[stage]:
                m[k] = weights[k]
            in_maps.append(m)
        res = run_bass_kernel_spmd(nc, in_maps, core_ids=list(range(NCORES)))
        x_cores = [res.results[ci]["xo"] for ci in range(NCORES)]
    return x_cores


def run_fused(x_cores, weights):
    consts = _const_arrays()
    nc = _get_prog("FUSED")
    in_maps = []
    for ci in range(NCORES):
        m = {"xi": x_cores[ci]}
        m.update(consts)
        for stage in ("A", "B", "C", "D"):
            for k in _STAGE_KEYS[stage]:
                m[k] = weights[k]
        in_maps.append(m)
    res = run_bass_kernel_spmd(nc, in_maps, core_ids=list(range(NCORES)))
    return [res.results[ci]["xo"] for ci in range(NCORES)]


def kernel(**inputs):
    x = np.ascontiguousarray(inputs["L0_attn_x"].astype(np.float32,
                                                        copy=False))
    xc = [np.ascontiguousarray(x[2 * c:2 * c + 2].reshape(T, D))
          for c in range(NCORES)]
    weights = _host_weights(inputs)
    xc = run_fused(xc, weights)
    out = np.stack([a.reshape(2, L, D) for a in xc], 0).reshape(16, L, D)
    return out.astype(np.float32, copy=False)
